# version 13
# baseline (speedup 1.0000x reference)
import numpy as np
import ml_dtypes
import concourse.bass as bass
import concourse.mybir as mybir
from concourse.bass_utils import run_bass_kernel_spmd
from contextlib import ExitStack

BF = ml_dtypes.bfloat16
F32 = mybir.dt.float32
BF16 = mybir.dt.bfloat16
OP = mybir.AluOpType
AF = mybir.ActivationFunctionType
AX = mybir.AxisListType.X

T = 4096
H = 1024
NS = 8
NT = 32
NCH = 8
NCORE = 8
ENG2SEM = {"tensor": "sT", "vector": "sV", "scalar": "sS"}


DMA_SEMS = ("dA0", "dA1", "dA2", "dU0", "dU1", "dP0", "dP1", "dW0", "dW1",
            "dN", "dNB", "dB1", "dB2", "dK", "dG0", "dG1", "dDBG",
            "dC0", "dC1", "dC2", "dC3", "dC4", "dC5", "dC6", "dC7", "dW2")


class Prog:
    ENGINES = ("sync", "gpsimd", "tensor", "vector", "scalar")

    def __init__(self, nc):
        self.nc = nc
        self.es = ExitStack()
        self.sems = {}
        self.cnt = {}
        for name in ("sT", "sV", "sS") + DMA_SEMS:
            self.sems[name] = self.es.enter_context(nc.semaphore(name))
            self.cnt[name] = 0
        self.streams = {e: [] for e in self.ENGINES}
        self.waited = {e: {} for e in self.ENGINES}
        self.dirtyW = {e: set() for e in self.ENGINES}
        self.dirtyR = {e: set() for e in self.ENGINES}

    def op(self, eng, fn, reads=(), writes=(), waits=(), inc="auto"):
        drain = False
        if eng in ("vector", "scalar", "gpsimd"):
            if (set(reads) & self.dirtyW[eng]) or (set(writes) & self.dirtyW[eng]) \
               or (set(writes) & self.dirtyR[eng]):
                drain = True
                self.dirtyW[eng] = set()
                self.dirtyR[eng] = set()
            self.dirtyW[eng] |= set(writes)
            self.dirtyR[eng] |= set(reads)
        if inc == "auto":
            inc = ENG2SEM[eng] if eng in ("vector", "scalar") else None
        tok = None
        amt = 1
        if inc is not None:
            self.cnt[inc] += amt
            tok = (inc, self.cnt[inc])
        own = ENG2SEM.get(eng)
        w = tuple((s, v) for (s, v) in [x for x in waits if x is not None]
                  if s != own)
        self.streams[eng].append((w, drain, fn, (inc, amt) if inc else None))
        return tok

    def dma(self, queue, dst, src, sem, waits=()):
        self.cnt[sem] += 16
        tok = (sem, self.cnt[sem])
        w = tuple((s, v) for (s, v) in [x for x in waits if x is not None])
        self.streams[queue].append(
            (w, False, lambda e, d=dst, s_=src: e.dma_start(d, s_), (sem, 16)))
        return tok

    def flush(self):
        with self.nc.Block(no_gpsimd_drain=True) as blk:
            for eng in self.ENGINES:
                ops = self.streams[eng]
                if not ops:
                    continue

                def make(ops_, eng_):
                    def emit(engine):
                        w8 = self.waited[eng_]
                        for waits, drain, fn, inc in ops_:
                            for (sn, val) in waits:
                                if w8.get(sn, 0) >= val:
                                    continue
                                engine.wait_ge(self.sems[sn], val)
                                w8[sn] = val
                            if drain:
                                engine.drain()
                            inst = fn(engine)
                            if inc is not None:
                                inst.then_inc(self.sems[inc[0]], inc[1])
                    return emit

                getattr(blk, eng)(make(list(ops), eng))
                self.streams[eng] = []
        for e in self.ENGINES:
            for s in ("sT", "sV", "sS"):
                self.waited[e][s] = max(self.waited[e].get(s, 0), self.cnt[s])
            self.dirtyW[e] = set()
            self.dirtyR[e] = set()


def ap3(handle, offset, dims):
    return bass.AP(handle, offset, [list(d) for d in dims])


def build_program(upto="D", dbg=()):
    nc = bass.Bass("TRN2", target_bir_lowering=False)
    DR = lambda name, shape, dt, kind="ExternalInput": nc.dram_tensor(
        name, shape, dt, kind=kind)

    x_h = DR("x", [T, H], F32)
    g_h = {k: DR(k, [H, H], F32) for k in ("wi", "wf", "wg", "wo")}
    ci_h = DR("ci", [128, 32], F32)
    cf_h = DR("cf", [128, 32], F32)
    gnw_h = DR("gnw", [128, 8], F32)
    fff_h = {}
    for f in ("q", "k", "v"):
        fff_h[f] = dict(
            w1=DR(f + "w1", [H, H], F32), b1=DR(f + "b1", [128, 8], F32),
            w2=DR(f + "w2", [H, H], F32), b2=DR(f + "b2", [128, H], F32),
            nwT=DR(f + "nwT", [H, 15], F32), nb=DR(f + "nb", [1, 15], F32))
    ident_h = DR("ident", [128, 128], BF16)
    epairs_h = DR("epairs", [128, H], BF16)
    iota16_h = DR("iota16", [128, 16], F32)
    onesrow_h = DR("onesrow", [1, 128], BF16)

    qkvT_h = [DR(n, [H, T], BF16, "Internal") for n in ("qTs", "kTs", "vTs")]
    wbf_h = {n: DR(n + "_bf", [H, H], BF16, "Internal")
             for n in ("wi", "wf", "wg", "wo", "qw1", "qw2", "kw1", "kw2",
                       "vw1", "vw2")}
    usp_h = DR("usp", [H, T], BF16, "Internal")
    out_h = DR("out", [T, H], F32, "ExternalOutput")
    dbg_h = {}
    if "xnT" in dbg:
        dbg_h["xnT"] = DR("dbg_xnT", [128, NS * T], BF16, "ExternalOutput")
    if "qkv" in dbg:
        for i, f in enumerate(("q", "k", "v")):
            dbg_h["qT_" + f] = DR("dbg_qT_" + f, [H, T], BF16, "ExternalOutput")
    if "u" in dbg:
        dbg_h["usp"] = DR("dbg_usp", [H, T], BF16, "ExternalOutput")
        dbg_h["rms"] = DR("dbg_rms", [128, 32], F32, "ExternalOutput")

    es = ExitStack()
    S_ = lambda name, shape, dt: es.enter_context(nc.sbuf_tensor(name, shape, dt))
    P_ = lambda name, shape, dt: es.enter_context(nc.psum_tensor(name, shape, dt))
    with es:
        A_xn = S_("A_xn", [128, NS * T], BF16)
        ident = S_("identS", [128, 128], BF16)
        epairs = S_("epairsS", [128, H], BF16)
        iota16 = S_("iota16S", [128, 16], F32)
        onesrow = S_("onesrowS", [1, 128], BF16)
        onescol = S_("onescolS", [128, 1], F32)
        epscol = S_("epscolS", [128, 1], F32)
        rms_all = S_("rms_all", [128, 32], F32)
        pA = P_("pA", [128, 512], F32)
        pB = P_("pB", [128, 512], F32)
        pC = P_("pC", [128, 512], F32)
        pD = P_("pD", [128, 512], F32)
        pS = P_("pS", [128, 512], F32)
        pT0 = P_("pT0", [128, 1024], BF16)
        pT1 = P_("pT1", [128, 1024], BF16)
        fb = [pA, pB, pC, pD]
        pTb = [pT0, pT1]

        P = Prog(nc)

        P.dma("sync", ident[:, :], ident_h.ap(), "dK")
        P.dma("sync", epairs[:, :], epairs_h.ap(), "dK")
        P.dma("sync", iota16[:, :], iota16_h.ap(), "dK")
        P.dma("sync", onesrow[:, :], onesrow_h.ap(), "dK")
        tok_K = ("dK", P.cnt["dK"])
        P.op("vector", lambda e: e.memset(onescol[:, :], 1.0), writes=("onescol",))
        P.op("vector", lambda e: e.memset(epscol[:, :], 1e-5), writes=("epscol",))

        esA = ExitStack()
        with esA:
            SA = lambda n, sh, dt: esA.enter_context(nc.sbuf_tensor(n, sh, dt))
            xt = [SA(f"xt{i}", [128, H], F32) for i in range(3)]
            dump = SA("dumpA", [128, H], F32)
            xnb = [SA(f"xnb{i}", [128, H], BF16) for i in range(2)]
            cS = [SA(f"cS{i}", [128, 1], F32) for i in range(2)]
            cQ = [SA(f"cQ{i}", [128, 1], F32) for i in range(2)]
            cNM = [SA(f"cNM{i}", [128, 1], F32) for i in range(2)]
            cM2 = [SA(f"cM2{i}", [128, 1], F32) for i in range(2)]
            cV = [SA(f"cV{i}", [128, 1], F32) for i in range(2)]
            cR = [SA(f"cR{i}", [128, 1], F32) for i in range(2)]
            tok_tr = [None, None]
            tok_ev = [None, None]
            tok_nrm = [None, None]
            tok_nrm3 = [None, None, None]
            prevA = None

            def tailA(pv):
                b = pv["par"]
                xi_ = pv["xi"]
                t = pv["t"]
                P.op("vector", lambda e, b_=b: e.reciprocal(cR[b_][:, 0:1], cR[b_][:, 0:1]),
                     reads=(f"cR{b}",), writes=(f"cR{b}",), waits=[pv["sqrt"]])
                tnorm = P.op("vector", lambda e, b_=b, x_=xi_: e.tensor_scalar(
                    xnb[b_][:, :], xt[x_][:, :], cNM[b_][:, 0:1], cR[b_][:, 0:1],
                    OP.add, OP.mult),
                    reads=(f"xt{xi_}", f"cNM{b}", f"cR{b}"), writes=(f"xnb{b}",),
                    waits=[tok_tr[b]])
                tok_nrm[b] = tnorm
                tok_nrm3[xi_] = tnorm
                tk = None
                for s in range(NS):
                    tk = P.op("tensor", lambda e, b_=b, s_=s: e.transpose(
                        pTb[b_][:, s_ * 128:(s_ + 1) * 128],
                        xnb[b_][:, s_ * 128:(s_ + 1) * 128], ident[:, :]),
                        waits=([tnorm, tok_ev[b], tok_K] if s == 0 else []),
                        inc=("sT" if s == NS - 1 else None))
                tok_tr[b] = tk
                dst = ap3(A_xn, t * 128, [(NS * T, 128), (T, 8), (1, 128)])
                tok_ev[b] = P.op("scalar", lambda e, b_=b, d=dst: e.copy(
                    d, pTb[b_][:, :].rearrange("p (b c) -> p b c", c=128)),
                    writes=("A_xn",), waits=[tk])

            dmas = {}
            XSEM = ("dA0", "dA1", "dA2")
            for t0 in range(2):
                dmas[t0] = P.dma("sync", xt[t0 % 3][:, :],
                                 x_h.ap()[t0 * 128:(t0 + 1) * 128, :],
                                 XSEM[t0 % 3], waits=[])
            for t in range(NT):
                bi = t % 2
                xi = t % 3
                if prevA is not None:
                    tailA(prevA)
                if t + 2 < NT:
                    tp2 = t + 2
                    dmas[tp2] = P.dma("sync", xt[tp2 % 3][:, :],
                                      x_h.ap()[tp2 * 128:(tp2 + 1) * 128, :],
                                      XSEM[tp2 % 3],
                                      waits=[tok_nrm3[tp2 % 3]])
                dq = dmas[t]
                xb = f"xt{xi}"
                P.op("vector", lambda e, b=xi, c=bi: e.reduce_sum(
                    cS[c][:, 0:1], xt[b][:, :], axis=AX),
                    reads=(xb,), writes=(f"cS{bi}",), waits=[dq])
                tsqx = P.op("scalar", lambda e, b=xi, c=bi: e.activation(
                    dump[:, :], xt[b][:, :], AF.Square,
                    accum_out=cQ[c][:, 0:1]),
                    reads=(xb,), writes=("dumpA", f"cQ{bi}"), waits=[dq])
                P.op("vector", lambda e, b=bi: e.tensor_scalar(
                    cNM[b][:, 0:1], cS[b][:, 0:1], -1.0 / H, None, OP.mult),
                    reads=(f"cS{bi}",), writes=(f"cNM{bi}",))
                P.op("vector", lambda e, b=bi: e.tensor_tensor(
                    cM2[b][:, 0:1], cNM[b][:, 0:1], cNM[b][:, 0:1], OP.mult),
                    reads=(f"cNM{bi}",), writes=(f"cM2{bi}",))
                tva = P.op("vector", lambda e, b=bi: e.scalar_tensor_tensor(
                    cV[b][:, 0:1], cQ[b][:, 0:1], 1.0 / H, cM2[b][:, 0:1],
                    OP.mult, OP.subtract),
                    reads=(f"cQ{bi}", f"cM2{bi}"), writes=(f"cV{bi}",),
                    waits=[tsqx])
                tsq = P.op("scalar", lambda e, b=bi: e.activation(
                    cR[b][:, 0:1], cV[b][:, 0:1], AF.Sqrt, bias=epscol[:, 0:1]),
                    reads=(f"cV{bi}", "epscolS"), writes=(f"cR{bi}",),
                    waits=[tva, tok_nrm[bi]])
                prevA = dict(par=bi, xi=xi, t=t, sqrt=tsq)
            tailA(prevA)
            if "xnT" in dbg:
                P.dma("gpsimd", dbg_h["xnT"].ap(), A_xn[:, :], "dDBG",
                      waits=[tok_ev[0], tok_ev[1]])
            P.flush()

        if upto == "A":
            return nc

        tok_qkv_store = None
        esB = ExitStack()
        with esB:
            SB = lambda n, sh, dt: esB.enter_context(nc.sbuf_tensor(n, sh, dt))
            A_big = SB("A_big", [128, NS * T], BF16)
            selT = SB("selT", [128, T], BF16)
            S_all = SB("S_allB", [128, 512], F32)
            wld = [SB(f"wldB{i}", [128, H], BF16) for i in range(2)]
            w2ld = [SB(f"w2ldB{i}", [128, H], BF16) for i in range(2)]
            cwl = [SB(f"cwl{i}", [128, H], F32) for i in range(4)]
            cwb = [SB(f"cwb{i}", [128, H], BF16) for i in range(4)]
            cast_q = []
            b1c = SB("b1c", [128, 8], F32)
            b2bf = SB("b2bf", [128, H], BF16)
            b2ld = SB("b2ld", [128, H], F32)
            nwbf = SB("nwbf", [128, 120], BF16)
            nwld = SB("nwld", [128, 120], F32)
            nbbf = SB("nbbf", [1, 15], BF16)
            nbld = SB("nbld", [1, 15], F32)
            Hf = [SB(f"Hf{i}", [128, 512], F32) for i in range(5)]
            qst = [SB(f"qst{i}", [128, T], BF16) for i in range(2)]
            tr1 = SB("tr1", [128, 32], F32)
            tr2 = SB("tr2", [128, 32], F32)
            tr3 = SB("tr3", [128, 32], F32)
            bb0 = SB("bb0", [128, 32], F32)
            bb1 = SB("bb1", [128, 32], F32)
            bb2 = SB("bb2", [128, 32], F32)
            leaf = SB("leafB", [128, 32], F32)
            ohb = SB("ohb", [128, 16], BF16)

            tz_sel = P.op("vector", lambda e: e.memset(selT[:, :], 0.0),
                          writes=("selT",))
            cast_st = {}
            CLSEM = ("dC0", "dC1", "dC2", "dC3")
            CSSEM = ("dC4", "dC5", "dC6", "dC7")
            cast_state = dict(ld=[None] * 4, cst=[None] * 4, stp=0)

            def cast_step():
                if not cast_q:
                    return
                name, s8 = cast_q.pop(0)
                src_h = g_h[name] if name in g_h else \
                    fff_h[name[0]]["w1" if name.endswith("1") else "w2"]
                i = cast_state["stp"]
                cast_state["stp"] += 1
                par = i % 4
                dl = P.dma("sync", cwl[par][:, :],
                           ap3(src_h, s8 * 128 * H, [(H, 128), (1, H)]),
                           CLSEM[par], waits=[cast_state["cst"][par]])
                eng = "vector" if i % 2 == 0 else "scalar"
                if eng == "vector":
                    ct = P.op("vector", lambda e, b=par: e.tensor_copy(
                        cwb[b][:, :], cwl[b][:, :]),
                        reads=(f"cwl{par}",), writes=(f"cwb{par}",),
                        waits=[dl, cast_state["ld"][par]])
                else:
                    ct = P.op("scalar", lambda e, b=par: e.copy(
                        cwb[b][:, :], cwl[b][:, :]),
                        reads=(f"cwl{par}",), writes=(f"cwb{par}",),
                        waits=[dl, cast_state["ld"][par]])
                cast_state["cst"][par] = ct
                last = P.dma("gpsimd",
                             ap3(wbf_h[name], s8 * 128 * H, [(H, 128), (1, H)]),
                             cwb[par][:, :], CSSEM[par], waits=[ct])
                cast_state["ld"][par] = last
                cast_st[name] = last

            def queue_wcast(name):
                for s8 in range(NS):
                    cast_q.append((name, s8))
                cast_st[name] = None

            def drain_wcast(name):
                while cast_st.get(name) is None:
                    cast_step()

            queue_wcast("qw1")
            queue_wcast("qw2")
            drain_wcast("qw1")
            drain_wcast("qw2")
            tok_qst = [None, None]
            tok_qev = [None, None]
            tok_w1c = [None, None]
            tok_w1te = [None, None]
            tok_w2te = [None, None]
            tok_hev = [None, None]
            tok_mmul = [None] * 5
            tok_hbank = [None, None]
            tok_kbank = [None, None]
            tok_tev_prev = None
            tok_tlf_prev = None
            tok_scmm_prev = None
            tok_w2mm_prev = None
            c1 = c2 = c3 = None
            for fi, f in enumerate(("q", "k", "v")):
                hh = fff_h[f]
                dst_h = qkvT_h[fi]
                nw_src = ap3(hh["nwT"], 0, [(15, 128), (15 * 128, 8), (1, 15)])
                t_nw = P.dma("sync", nwld[:, :].rearrange("p (b c) -> p b c", c=15),
                             nw_src, "dN", waits=[c1])
                t_nb = P.dma("sync", nbld[:, :], hh["nb"].ap(), "dNB", waits=[c2])
                t_b1 = P.dma("sync", b1c[:, :], hh["b1"].ap(), "dB1",
                             waits=[tok_hev[0], tok_hev[1]])
                t_b2 = P.dma("sync", b2ld[:, :], hh["b2"].ap(), "dB2", waits=[c3])
                c1 = P.op("vector", lambda e: e.tensor_copy(nwbf[:, :], nwld[:, :]),
                          reads=("nwld",), writes=("nwbf",),
                          waits=[t_nw, tok_scmm_prev])
                c2 = P.op("vector", lambda e: e.tensor_copy(nbbf[:, :], nbld[:, :]),
                          reads=("nbld",), writes=("nbbf",),
                          waits=[t_nb, tok_scmm_prev])
                c3 = P.op("vector", lambda e: e.tensor_copy(b2bf[:, :], b2ld[:, :]),
                          reads=("b2ld",), writes=("b2bf",),
                          waits=[t_b2, tok_w2mm_prev])
                tok_sc = None
                for t in range(NT):
                    for s in range(NS):
                        P.op("tensor", lambda e, t_=t, s_=s: e.matmul(
                            pS[:, 16 * t_:16 * t_ + 15],
                            A_xn[:, s_ * T + t_ * 128:s_ * T + t_ * 128 + 128],
                            nwbf[:, s_ * 15:s_ * 15 + 15],
                            start=(s_ == 0), stop=False),
                            waits=([c1, c2, tok_K, tok_tev_prev]
                                   if (t == 0 and s == 0) else []),
                            inc=None)
                    tok_sc = P.op("tensor", lambda e, t_=t: e.matmul(
                        pS[:, 16 * t_:16 * t_ + 15], onesrow[:, :], nbbf[:, :],
                        start=False, stop=True),
                        inc=("sT" if t == NT - 1 else None))
                tok_scmm_prev = tok_sc
                tev = P.op("scalar", lambda e: e.copy(S_all[:, :], pS[:, :]),
                           writes=("S_allB",), waits=[tok_sc, tok_tlf_prev])
                tok_tev_prev = tev
                col = lambda j: ap3(S_all, j, [(512, 128), (16, 32)])
                V = "vector"

                def tt(d, a, b_, o, extra_w=()):
                    return P.op(V, lambda e: e.tensor_tensor(d[1], a[1], b_[1], o),
                                reads=(a[0], b_[0]), writes=(d[0],), waits=extra_w)

                def ge(d, a, extra_w=()):
                    return P.op(V, lambda e: e.tensor_scalar(
                        d[1], a[1], 0.0, None, OP.is_ge),
                        reads=(a[0],), writes=(d[0],), waits=extra_w)

                SA_ = lambda j: ("S_allB", col(j))
                TR1 = ("tr1", tr1[:, :]); TR2 = ("tr2", tr2[:, :])
                TR3 = ("tr3", tr3[:, :])
                B0 = ("bb0", bb0[:, :]); B1 = ("bb1", bb1[:, :])
                B2 = ("bb2", bb2[:, :])
                ge(B0, SA_(0), extra_w=[tev])

                def lerp(dst, lo, hi, bit):
                    tt(dst, hi, lo, OP.subtract)
                    tt(dst, dst, bit, OP.mult)
                    tt(dst, dst, lo, OP.add)
                lerp(TR1, SA_(1), SA_(2), B0)
                ge(B1, TR1)
                lerp(TR1, SA_(3), SA_(4), B1)
                lerp(TR2, SA_(5), SA_(6), B1)
                lerp(TR2, TR1, TR2, B0)
                ge(B2, TR2)
                lerp(TR1, SA_(7), SA_(8), B2)
                lerp(TR2, SA_(9), SA_(10), B2)
                lerp(TR2, TR1, TR2, B1)
                lerp(TR1, SA_(11), SA_(12), B2)
                lerp(TR3, SA_(13), SA_(14), B2)
                lerp(TR3, TR1, TR3, B1)
                lerp(TR3, TR2, TR3, B0)
                ge(TR3, TR3)
                P.op(V, lambda e: e.scalar_tensor_tensor(
                    leaf[:, :], bb0[:, :], 2.0, bb1[:, :], OP.mult, OP.add),
                    reads=("bb0", "bb1"), writes=("leafB",))
                P.op(V, lambda e: e.scalar_tensor_tensor(
                    leaf[:, :], leaf[:, :], 2.0, bb2[:, :], OP.mult, OP.add),
                    reads=("leafB", "bb2"), writes=("leafB",))
                tlf = P.op(V, lambda e: e.scalar_tensor_tensor(
                    leaf[:, :], leaf[:, :], 2.0, tr3[:, :], OP.mult, OP.add),
                    reads=("leafB", "tr3"), writes=("leafB",))
                tok_tlf_prev = tlf
                tok_ohT = None
                tok_selev = [None, None]
                first_sel = True
                for t in range(NT):
                    gi = (t // 8) % 2
                    toh = P.op(V, lambda e, t_=t: e.tensor_scalar(
                        ohb[:, :], iota16[:, :], leaf[:, t_:t_ + 1], None,
                        OP.is_equal),
                        reads=("iota16S", "leafB"), writes=("ohb",),
                        waits=[tlf, tok_ohT])
                    tok_ohT = P.op("tensor", lambda e, t_=t, g=gi: e.transpose(
                        pTb[g][0:16, (t_ % 8) * 128:(t_ % 8) * 128 + 128],
                        ohb[:, :], ident[:, :]),
                        waits=[toh] + ([tok_selev[gi]] if t % 8 == 0 else []),
                        inc="sT")
                    if t % 8 == 7:
                        t0 = t - 7
                        w_extra = [tok_w2mm_prev, tz_sel] if first_sel else []
                        first_sel = False
                        tok_selev[gi] = P.op(V, lambda e, t0_=t0, g=gi: e.tensor_copy(
                            selT[0:16, t0_ * 128:(t0_ + 8) * 128], pTb[g][0:16, :]),
                            writes=("selT",), waits=[tok_ohT] + w_extra)
                tok_sel = tok_selev[1]
                for wname in (("kw1", "kw2", "wi") if fi == 0 else
                              ("vw1", "vw2", "wf") if fi == 1 else
                              ("wg", "wo")):
                    queue_wcast(wname)
                last_h = None
                w1name = f + "w1"
                drain_wcast(w1name)
                for p in range(NS):
                    bi = p % 2
                    cast_step()
                    cast_step()
                    wsrc = ap3(wbf_h[w1name], p * 128,
                               [(H, 128), (128 * H, 8), (1, 128)])
                    tok_w1c[bi] = P.dma(
                        "sync", wld[bi][:, :].rearrange("p (b c) -> p b c", c=128),
                        wsrc, "dW0" if bi == 0 else "dW1",
                        waits=[tok_w1te[bi], tok_w2te[bi], cast_st[w1name]])

                    def h_group(p_, ch_, hslot):
                        bi_ = p_ % 2
                        par = (p_ * NCH + ch_) % 2
                        mb = fb[par]
                        tkh = None
                        for s in range(NS):
                            tkh = P.op("tensor", lambda e, s_=s, c_=ch_, b=bi_, m=mb: e.matmul(
                                m[:, :], wld[b][:, s_ * 128:(s_ + 1) * 128],
                                A_xn[:, s_ * T + c_ * 512:s_ * T + c_ * 512 + 512],
                                start=(s_ == 0), stop=(s_ == NS - 1)),
                                waits=([tok_w1c[bi_], tok_hbank[par]] if s == 0 else []),
                                inc=("sT" if s == NS - 1 else None))
                        t_he = P.op("scalar", lambda e, q_=p_, hs=hslot, m=mb: e.activation(
                            Hf[hs][:, :], m[:, :], AF.Relu, bias=b1c[:, q_:q_ + 1]),
                            reads=("b1c",), writes=(f"Hf{hslot}",),
                            waits=[tkh, t_b1, tok_mmul[hslot]])
                        tok_hev[par] = t_he
                        tok_hbank[par] = t_he
                        return tkh, t_he

                    def m_group(p_, ch_, hslot):
                        par = (p_ * NCH + ch_) % 2
                        kb = fb[2 + par]
                        t_m = P.op("tensor", lambda e, q_=p_, c_=ch_, k=kb: e.matmul(
                            k[:, :], epairs[:, q_ * 128:(q_ + 1) * 128],
                            selT[:, c_ * 512:c_ * 512 + 512],
                            start=True, stop=True),
                            waits=[tok_sel, tok_kbank[par]], inc="sT")
                        t_he = tok_hev_slot[hslot]
                        t_mm = P.op(V, lambda e, q_=p_, c_=ch_, hs=hslot, k=kb: e.tensor_tensor(
                            A_big[:, q_ * T + c_ * 512:q_ * T + c_ * 512 + 512],
                            Hf[hs][:, :], k[:, :], OP.mult),
                            reads=(f"Hf{hslot}",), writes=("A_big",),
                            waits=[t_he, t_m])
                        tok_mmul[hslot] = t_mm
                        tok_kbank[par] = t_mm
                        return t_mm

                    if p == 0:
                        L = 5
                        tok_hev_slot = {}
                        t_h = None
                        for ch in range(NCH):
                            if ch >= L:
                                last_h = m_group(p, ch - L, (ch - L) % L)
                            t_h, t_he = h_group(p, ch, ch % L)
                            tok_hev_slot[ch % L] = t_he
                        for ch in range(NCH - L, NCH):
                            last_h = m_group(p, ch, ch % L)
                    else:
                        tok_hev_slot = {}
                        t_h = None
                        for ch in range(NCH):
                            t_h, t_he = h_group(p, ch, ch % 2)
                            tok_hev_slot[ch % 2] = t_he
                            last_h = m_group(p, ch, ch % 2)
                    tok_w1te[bi] = t_h
                tq = None
                w2name = f + "w2"
                drain_wcast(w2name)
                for dch in range(NS):
                    bi = dch % 2
                    cast_step()
                    wsrc = ap3(wbf_h[w2name], dch * 128,
                               [(H, 128), (128 * H, 8), (1, 128)])
                    tok_w2c2 = P.dma(
                        "sync", w2ld[bi][:, :].rearrange("p (b c) -> p b c", c=128),
                        wsrc, "dW0" if bi == 0 else "dW1",
                        waits=[tok_w2te[bi], cast_st[w2name], tok_w1c[bi]])
                    sp = dch % 2
                    for ch in range(NCH):
                        par = (dch * NCH + ch) % 2
                        mb = fb[par]
                        for p in range(NS):
                            P.op("tensor", lambda e, p_=p, ch_=ch, b=bi, m=mb: e.matmul(
                                m[:, :], w2ld[b][:, p_ * 128:(p_ + 1) * 128],
                                A_big[:, p_ * T + ch_ * 512:p_ * T + ch_ * 512 + 512],
                                start=(p_ == 0), stop=False),
                                waits=([tok_w2c2, last_h, c3, tok_qev[par]]
                                       if p == 0 else []), inc=None)
                        tq = P.op("tensor", lambda e, ch_=ch, d_=dch, m=mb: e.matmul(
                            m[:, :], b2bf[:, d_ * 128:d_ * 128 + 128],
                            selT[:, ch_ * 512:ch_ * 512 + 512],
                            start=False, stop=True), inc="sT")
                        tok_qev[par] = P.op("scalar", lambda e, ch_=ch, s=sp, m=mb: e.copy(
                            qst[s][:, ch_ * 512:ch_ * 512 + 512], m[:, :]),
                            writes=(f"qst{sp}",),
                            waits=[tq] + ([tok_qst[sp]] if ch == 0 else []))
                    tok_w2te[bi] = tq
                    tok_qst[sp] = P.dma(
                        "gpsimd", ap3(dst_h, dch * 128 * T, [(T, 128), (1, T)]),
                        qst[sp][:, :], "dG0" if sp == 0 else "dG1",
                        waits=[tok_qev[0], tok_qev[1]])
                tok_w2mm_prev = tq
                if "qkv" in dbg:
                    P.dma("gpsimd", dbg_h["qT_" + f].ap(), dst_h.ap(), "dDBG",
                          waits=[tok_qst[0], tok_qst[1]])
            while cast_q:
                cast_step()
            P.flush()
        tok_B_st0 = ("dG0", P.cnt["dG0"])
        tok_B_st1 = ("dG1", P.cnt["dG1"])

        if upto == "B":
            return nc

        esC = ExitStack()
        with esC:
            SC = lambda n, sh, dt: esC.enter_context(nc.sbuf_tensor(n, sh, dt))
            C1 = SC("C1", [128, T + 4], F32)
            C2 = SC("C2", [128, T + 4], F32)
            C3 = SC("C3", [128, T], F32)
            C4 = SC("C4", [128, T], F32)
            C5 = SC("C5", [128, T], F32)
            sqb = SC("sqb", [128, T], BF16)
            wldC = [SC(f"wldC{i}", [128, H], BF16) for i in range(3)]
            cicol = SC("cicol", [128, 32], F32)
            cfcol = SC("cfcol", [128, 32], F32)
            gnwc = SC("gnwc", [128, 8], F32)
            ssq = SC("ssq", [128, 32], F32)
            onescol_bf = SC("onescol_bf", [128, 1], BF16)
            ust = [SC(f"ust{i}", [128, T], BF16) for i in range(2)]

            P.dma("sync", cicol[:, :], ci_h.ap(), "dK")
            P.dma("sync", cfcol[:, :], cf_h.ap(), "dK")
            P.dma("sync", gnwc[:, :], gnw_h.ap(), "dK")
            tok_KC = ("dK", P.cnt["dK"])
            d_ci = d_cf = d_gn = tok_KC
            tz1 = P.op("vector", lambda e: e.memset(C1[:, 0:3], 0.0), writes=("C1",))
            tz2 = P.op("vector", lambda e: e.memset(C2[:, 0:3], 0.0), writes=("C2",))
            tz3 = P.op("vector", lambda e: e.memset(ssq[:, :], 0.0), writes=("ssq",))
            tzo = P.op("vector", lambda e: e.memset(onescol_bf[:, :], 1.0),
                       writes=("onescol_bf",))

            gates = [("wi", C1, 3, "C1"), ("wf", C2, 3, "C2"), ("wg", C3, 0, "C3")]
            st_c = dict(wload=[None, None, None], wte=[None, None, None],
                        bank=[None, None, None, None],
                        convi=None, convf=None, u=None, scan=None, sq=None,
                        ssqmm=None, udma=[None, None], ssqacc=None)
            evs = {}

            WSEM = ("dW0", "dW1", "dW2")

            def emit_wload(s, gidx):
                gname = gates[gidx][0]
                par = (s * 3 + gidx) % 3
                wsrc = ap3(wbf_h[gname], s * 128, [(H, 128), (128 * H, 8), (1, 128)])
                st_c["wload"][par] = P.dma(
                    "sync", wldC[par][:, :].rearrange("p (b c) -> p b c", c=128),
                    wsrc, WSEM[par],
                    waits=[st_c["wte"][par], cast_st[gname]])

            def emit_group(s, gidx, ch):
                par = (s * 3 + gidx) % 3
                bpar = (gidx * NCH + ch) % 4
                mb = fb[bpar]
                dstbuf, ofs, bufname = gates[gidx][1], gates[gidx][2], gates[gidx][3]
                tg = None
                for hs in range(NS):
                    tg = P.op("tensor", lambda e, h_=hs, c_=ch, b=par, m=mb: e.matmul(
                        m[:, :], wldC[b][:, h_ * 128:(h_ + 1) * 128],
                        A_xn[:, h_ * T + c_ * 512:h_ * T + c_ * 512 + 512],
                        start=(h_ == 0), stop=(h_ == NS - 1)),
                        waits=([st_c["wload"][par], st_c["bank"][bpar]]
                               if hs == 0 else []),
                        inc=("sT" if hs == NS - 1 else None))
                st_c["wte"][par] = tg
                if gidx == 0:
                    dfree = [st_c["convi"], tz1]
                elif gidx == 1:
                    dfree = [st_c["convf"], tz2]
                else:
                    dfree = [st_c["u"]]
                tev = P.op("scalar", lambda e, d=dstbuf, o=ofs, c_=ch, m=mb: e.copy(
                    d[:, o + c_ * 512:o + c_ * 512 + 512], m[:, :]),
                    writes=(bufname,), waits=[tg] + dfree)
                st_c["bank"][bpar] = tev
                evs.setdefault(s, {}).setdefault(gidx, []).append(tev)

            def emit_ssq(s):
                tmm = None
                for t in range(NT):
                    tmm = P.op("tensor", lambda e, t_=t: e.matmul(
                        pS[:, t_:t_ + 1], sqb[:, t_ * 128:(t_ + 1) * 128],
                        onescol_bf[:, :], start=True, stop=True),
                        waits=([st_c["sq"], st_c["ssqacc"], tzo] if t == 0 else []),
                        inc=("sT" if t == NT - 1 else None))
                st_c["ssqmm"] = tmm

            for gidx in range(3):
                emit_wload(0, gidx)
                for ch in range(NCH):
                    emit_group(0, gidx, ch)

            for s in range(NS):
                if s + 1 < NS:
                    for gidx in range(3):
                        emit_wload(s + 1, gidx)
                if s >= 1:
                    emit_ssq(s - 1)
                    st_c["ssqacc"] = P.op("vector", lambda e: e.tensor_tensor(
                        ssq[:, :], ssq[:, :], pS[:, 0:32], OP.add),
                        reads=("ssq",), writes=("ssq",),
                        waits=[st_c["ssqmm"], tz3])
                ev_i = evs[s][0]
                ev_f = evs[s][1]
                ev_g = evs[s][2]
                w0 = ev_i + [d_ci]
                P.op("vector", lambda e, s_=s: e.tensor_scalar(
                    C4[:, :], C1[:, 0:T], cicol[:, 4 * s_:4 * s_ + 1], None, OP.mult),
                    reads=("C1", "cicol"), writes=("C4",), waits=w0)
                for k in range(1, 4):
                    P.op("vector", lambda e, s_=s, k_=k: e.scalar_tensor_tensor(
                        C4[:, :], C1[:, k_:k_ + T], cicol[:, 4 * s_ + k_:4 * s_ + k_ + 1],
                        C4[:, :], OP.mult, OP.add),
                        reads=("C1", "C4", "cicol"), writes=("C4",))
                st_c["convi"] = ("sV", P.cnt["sV"])
                w0 = ev_f + [d_cf, st_c["sq"]]
                P.op("vector", lambda e, s_=s: e.tensor_scalar(
                    C5[:, :], C2[:, 0:T], cfcol[:, 4 * s_:4 * s_ + 1], None, OP.mult),
                    reads=("C2", "cfcol"), writes=("C5",), waits=w0)
                for k in range(1, 4):
                    P.op("vector", lambda e, s_=s, k_=k: e.scalar_tensor_tensor(
                        C5[:, :], C2[:, k_:k_ + T], cfcol[:, 4 * s_ + k_:4 * s_ + k_ + 1],
                        C5[:, :], OP.mult, OP.add),
                        reads=("C2", "C5", "cfcol"), writes=("C5",))
                st_c["convf"] = ("sV", P.cnt["sV"])
                t_sg = P.op("scalar", lambda e: e.activation(
                    C3[:, :], C3[:, :], AF.Silu),
                    reads=("C3",), writes=("C3",), waits=ev_g)
                t_silu = P.op("scalar", lambda e: e.activation(
                    C4[:, :], C4[:, :], AF.Silu),
                    reads=("C4",), writes=("C4",), waits=[st_c["convi"]])
                if s + 1 < NS:
                    for ch in range(NCH):
                        emit_group(s + 1, 0, ch)
                t_sig = P.op("scalar", lambda e: e.activation(
                    C5[:, :], C5[:, :], AF.Sigmoid),
                    reads=("C5",), writes=("C5",), waits=[st_c["convf"]])
                if s + 1 < NS:
                    for ch in range(NCH):
                        emit_group(s + 1, 1, ch)
                t_ig = P.op("vector", lambda e: e.scalar_tensor_tensor(
                    C4[:, :], C5[:, :], 1.0, C4[:, :], OP.subtract, OP.mult),
                    reads=("C5", "C4"), writes=("C4",), waits=[t_sig, t_silu])
                st_c["scan"] = P.op("vector", lambda e: e.tensor_tensor_scan(
                    C5[:, :], C5[:, :], C4[:, :], 0.0, OP.mult, OP.subtract),
                    reads=("C5", "C4"), writes=("C5",), waits=[t_ig])
                st_c["sq"] = P.op("scalar", lambda e: e.activation(
                    sqb[:, :], C5[:, :], AF.Square),
                    reads=("C5",), writes=("sqb",),
                    waits=[st_c["scan"], st_c["ssqmm"]])
                sp = s % 2
                st_c["u"] = P.op("vector", lambda e, s_=s, p_=sp: e.scalar_tensor_tensor(
                    ust[p_][:, :], C5[:, :], gnwc[:, s_:s_ + 1], C3[:, :],
                    OP.mult, OP.mult),
                    reads=("C5", "C3", "gnwc"), writes=(f"ust{sp}",),
                    waits=[st_c["scan"], t_sg, d_gn, st_c["udma"][sp]])
                st_c["udma"][sp] = P.dma(
                    "gpsimd", ap3(usp_h, s * 128 * T, [(T, 128), (1, T)]),
                    ust[sp][:, :], "dG0" if sp == 0 else "dG1",
                    waits=[st_c["u"]])
                if s + 1 < NS:
                    for ch in range(NCH):
                        emit_group(s + 1, 2, ch)
            emit_ssq(NS - 1)
            st_c["ssqacc"] = P.op("vector", lambda e: e.tensor_tensor(
                ssq[:, :], ssq[:, :], pS[:, 0:32], OP.add),
                reads=("ssq",), writes=("ssq",), waits=[st_c["ssqmm"], tz3])
            t_rs = P.op("scalar", lambda e: e.activation(
                rms_all[:, :], ssq[:, :], AF.Sqrt, bias=epscol[:, 0:1],
                scale=1.0 / H),
                reads=("ssq", "epscolS"), writes=("rms_all",),
                waits=[st_c["ssqacc"]])
            P.op("vector", lambda e: e.reciprocal(rms_all[:, :], rms_all[:, :]),
                 reads=("rms_all",), writes=("rms_all",), waits=[t_rs])
            if "u" in dbg:
                P.dma("gpsimd", dbg_h["usp"].ap(), usp_h.ap(), "dDBG",
                      waits=[st_c["udma"][0], st_c["udma"][1]])
                P.dma("gpsimd", dbg_h["rms"].ap(), rms_all[:, :], "dDBG")
            P.flush()
        tok_C_st0 = ("dG0", P.cnt["dG0"])
        tok_C_st1 = ("dG1", P.cnt["dG1"])

        if upto == "C":
            return nc

        esD = ExitStack()
        with esD:
            SD = lambda n, sh, dt: esD.enter_context(nc.sbuf_tensor(n, sh, dt))
            A_big = SD("A_bigD", [128, NS * T], BF16)
            qb = [SD(f"qbD{i}", [128, H], BF16) for i in range(2)]
            ptl = [SD(f"ptl{i}", [128, H], F32) for i in range(2)]
            otD = [SD(f"otD{i}", [128, H], F32) for i in range(2)]
            mx8 = [SD(f"mx8{i}", [128, 8], F32) for i in range(2)]
            ngm = [SD(f"ngm{i}", [128, 1], F32) for i in range(2)]
            rsc = [SD(f"rsc{i}", [128, 1], F32) for i in range(2)]
            riv = [SD(f"riv{i}", [128, 1], F32) for i in range(2)]

            tok_addD = [None, None]
            tok_odmaD = [None, None]
            tok_sev = [None, None]
            esC2 = ExitStack()
            with esC2:
                SC2 = lambda n, sh, dt: esC2.enter_context(nc.sbuf_tensor(n, sh, dt))
                wo_bf = SC2("wo_sb", [128, NS * H], BF16)
                ub = [SC2(f"ub{i}", [128, H], BF16) for i in range(2)]
                vbD = SC2("vbD", [128, T], BF16)
                for s in range(NS):
                    P.dma("sync", wo_bf[:, s * H:(s + 1) * H],
                          ap3(wbf_h["wo"], s * 128 * H, [(H, 128), (1, H)]),
                          "dK", waits=[cast_st["wo"]] if s == 0 else [])
                tok_wo = [("dK", P.cnt["dK"])]
                for s in range(NS):
                    P.dma("sync", A_xn[:, s * T:(s + 1) * T],
                          ap3(qkvT_h[1], s * 128 * T, [(T, 128), (1, T)]),
                          "dK", waits=[tok_B_st0, tok_B_st1])
                tok_KD = ("dK", P.cnt["dK"])
                tok_umm = [None, None]
                tok_oev = [None, None]
                tok_vev = [None, None]
                tok_vtr = None
                vstate = dict(ds=0, kt=0, dv=None)

                def v_step():
                    ds = vstate["ds"]
                    if ds >= NS:
                        return
                    kt0 = vstate["kt"]
                    if kt0 == 0:
                        vstate["dv"] = P.dma(
                            "sync", vbD[:, :],
                            ap3(qkvT_h[2], ds * 128 * T, [(T, 128), (1, T)]),
                            "dNB", waits=[vstate.get("prev_tr"), tok_B_st0, tok_B_st1])
                    g = (kt0 // 8) % 2
                    tk = None
                    for kt in range(kt0, kt0 + 8):
                        tk = P.op("tensor", lambda e, k_=kt, g_=g: e.transpose(
                            pTb[g_][:, (k_ % 8) * 128:(k_ % 8) * 128 + 128],
                            vbD[:, k_ * 128:(k_ + 1) * 128], ident[:, :]),
                            waits=([vstate["dv"], tok_vev[g]] if kt == kt0 else []),
                            inc=("sT" if kt == kt0 + 7 else None))
                    dstv = ap3(A_big, kt0 * H + ds * 128,
                               [(NS * T, 128), (H, 8), (1, 128)])
                    tok_vev[g] = P.op("vector", lambda e, d=dstv, g_=g: e.tensor_copy(
                        d, pTb[g_][:, :].rearrange("p (b c) -> p b c", c=128)),
                        writes=("A_bigD",), waits=[tk])
                    vstate["kt"] = kt0 + 8
                    if vstate["kt"] >= NT:
                        vstate["kt"] = 0
                        vstate["ds"] = ds + 1
                        vstate["prev_tr"] = tk

                for t in range(NT):
                    par = t % 2
                    du = P.dma("sync", ub[par][:, :].rearrange("p (b c) -> p b c", c=128),
                               ap3(usp_h, t * 128, [(T, 128), (128 * T, 8), (1, 128)]),
                               "dU0" if par == 0 else "dU1",
                               waits=[tok_umm[par], tok_C_st0, tok_C_st1])
                    dx = P.dma("sync", ptl[par][:, :],
                               x_h.ap()[t * 128:(t + 1) * 128, :],
                               "dA0" if par == 0 else "dA1", waits=[tok_addD[par]])
                    for half in range(2):
                        mb = fb[half]
                        for s in range(NS):
                            tg = P.op("tensor", lambda e, s_=s, h_=half, b=par, m=mb: e.matmul(
                                m[:, :], ub[b][:, s_ * 128:(s_ + 1) * 128],
                                wo_bf[:, s_ * H + h_ * 512:s_ * H + h_ * 512 + 512],
                                start=(s_ == 0), stop=(s_ == NS - 1)),
                                waits=([du, tok_oev[half]] + tok_wo if s == 0 else []),
                                inc=("sT" if s == NS - 1 else None))
                        tok_oev[half] = P.op("scalar", lambda e, h_=half, b=par, m=mb: e.copy(
                            otD[b][:, h_ * 512:h_ * 512 + 512], m[:, :]),
                            writes=(f"otD{par}",),
                            waits=[tg, tok_odmaD[par]])
                    tok_umm[par] = tg
                    tok_addD[par] = P.op("vector", lambda e, t_=t, b=par: e.scalar_tensor_tensor(
                        otD[b][:, :], otD[b][:, :], rms_all[:, t_:t_ + 1],
                        ptl[b][:, :], OP.mult, OP.add),
                        reads=(f"otD{par}", f"ptl{par}", "rms_all"),
                        writes=(f"otD{par}",),
                        waits=[tok_oev[0], tok_oev[1], dx])
                    tok_odmaD[par] = P.dma(
                        "gpsimd", ap3(out_h, t * 128 * H, [(H, 128), (1, H)]),
                        otD[par][:, :], "dG0" if par == 0 else "dG1",
                        waits=[tok_addD[par]])
                    v_step()
                while vstate["ds"] < NS:
                    v_step()
                tok_sev[0] = tok_oev[0]
                tok_sev[1] = tok_oev[1]
            tok_C2_st0 = ("dG0", P.cnt["dG0"])
            tok_C2_st1 = ("dG1", P.cnt["dG1"])
            S_sb = SD("S_sb", [128, T], F32)
            P_bf = [SD(f"P_bf{i}", [128, T], BF16) for i in range(2)]
            PT_sb = SD("PT_sb", [128, T], BF16)

            tok_smm = [None, None]
            tok_exp_par = [None, None]
            tok_ptr_par = [None, None]
            tok_ptev = [tok_vev[0], tok_vev[1]]
            tok_pv = [None, None]
            tok_pvlast = [None]
            tok_riv = [None, None]
            tok_opv = [None, None]
            tok_chmax_prev = None

            def tail_tr(pv):
                b = pv["par"]
                tk = None
                for kt in range(NT):
                    g = (kt // 8) % 2
                    tk = P.op("tensor", lambda e, k_=kt, g_=g, b_=b: e.transpose(
                        pTb[g_][:, (k_ % 8) * 128:(k_ % 8) * 128 + 128],
                        P_bf[b_][:, k_ * 128:(k_ + 1) * 128], ident[:, :]),
                        waits=([pv["exp"], tok_ptev[g]] if kt % 8 == 0 else []),
                        inc=("sT" if kt % 8 == 7 else None))
                    if kt % 8 == 7:
                        kt0 = kt - 7
                        tok_ptev[g] = P.op("vector", lambda e, k0=kt0, g_=g: e.tensor_copy(
                            PT_sb[:, k0 * 128:(k0 + 8) * 128], pTb[g_][:, :]),
                            writes=("PT_sb",), waits=[tk, tok_pvlast[0]])
                tok_ptr_par[b] = tk

            def tail_pv(pv):
                b = pv["par"]
                tt = pv["t"]
                for half in range(2):
                    mb = fb[2 + half]
                    tg = None
                    for kt in range(NT):
                        tg = P.op("tensor", lambda e, k_=kt, h_=half, m=mb: e.matmul(
                            m[:, :], PT_sb[:, k_ * 128:(k_ + 1) * 128],
                            A_big[:, k_ * H + h_ * 512:k_ * H + h_ * 512 + 512],
                            start=(k_ == 0), stop=(k_ == NT - 1)),
                            waits=([tok_ptev[0], tok_ptev[1], tok_pv[half]]
                                   if kt == 0 else []),
                            inc=("sT" if kt == NT - 1 else None))
                    tok_pvlast[0] = tg
                    tok_pv[half] = P.op("scalar", lambda e, h_=half, b_=b, m=mb: e.activation(
                        otD[b_][:, h_ * 512:h_ * 512 + 512], m[:, :], AF.Copy,
                        scale=riv[b_][:, 0:1]),
                        reads=(f"riv{b}",), writes=(f"otD{b}",),
                        waits=[tg, pv["riv"], tok_odmaD[b]])
                    tok_opv[b] = tok_pv[half]
                tok_addD[b] = P.op("vector", lambda e, b_=b: e.tensor_tensor(
                    otD[b_][:, :], otD[b_][:, :], ptl[b_][:, :], OP.add),
                    reads=(f"otD{b}", f"ptl{b}"), writes=(f"otD{b}",),
                    waits=[tok_pv[0], tok_pv[1], pv["dp"]])
                tok_odmaD[b] = P.dma(
                    "gpsimd", ap3(out_h, tt * 128 * H, [(H, 128), (1, H)]),
                    otD[b][:, :], "dG0" if b == 0 else "dG1",
                    waits=[tok_addD[b]])

            prev = None
            for t in range(NT):
                par = t % 2
                if prev is not None:
                    tail_tr(prev)
                dq = P.dma("sync", qb[par][:, :].rearrange("p (b c) -> p b c", c=128),
                           ap3(qkvT_h[0], t * 128, [(T, 128), (128 * T, 8), (1, 128)]),
                           "dA0" if par == 0 else "dA1",
                           waits=[tok_smm[par], tok_B_st0, tok_B_st1])
                dp = P.dma("sync", ptl[par][:, :],
                           ap3(out_h, t * 128 * H, [(H, 128), (1, H)]),
                           "dP0" if par == 0 else "dP1",
                           waits=[tok_addD[par], tok_C2_st0, tok_C2_st1])
                t_chmax = None
                for kc in range(NCH):
                    sb = fb[kc % 2]
                    tg = None
                    for s in range(NS):
                        tg = P.op("tensor", lambda e, s_=s, k_=kc, b=par, m=sb: e.matmul(
                            m[:, :], qb[b][:, s_ * 128:(s_ + 1) * 128],
                            A_xn[:, s_ * T + k_ * 512:s_ * T + k_ * 512 + 512],
                            start=(s_ == 0), stop=(s_ == NS - 1)),
                            waits=([dq, tok_sev[kc % 2], tok_KD] if s == 0 else []),
                            inc=("sT" if s == NS - 1 else None))
                    tok_smm[par] = tg
                    tok_sev[kc % 2] = P.op("scalar", lambda e, k_=kc, m=sb: e.copy(
                        S_sb[:, k_ * 512:k_ * 512 + 512], m[:, :]),
                        writes=("S_sb",),
                        waits=[tg] + ([tok_chmax_prev] if kc == 0 else []))
                    t_chmax = P.op("vector", lambda e, k_=kc, b=par: e.reduce_max(
                        mx8[b][:, k_:k_ + 1], S_sb[:, k_ * 512:k_ * 512 + 512],
                        axis=AX, negate=True),
                        reads=("S_sb",), writes=(f"mx8{par}",),
                        waits=[tok_sev[kc % 2]])
                tok_chmax_prev = t_chmax
                t_rmx = P.op("vector", lambda e, b=par: e.tensor_reduce(
                    ngm[b][:, 0:1], mx8[b][:, :], AX, OP.min),
                    reads=(f"mx8{par}",), writes=(f"ngm{par}",),
                    waits=[t_chmax, tok_exp_par[par]])
                t_bias = P.op("vector", lambda e, b=par: e.tensor_scalar(
                    ngm[b][:, 0:1], ngm[b][:, 0:1], 1.0 / 32.0, None, OP.mult),
                    reads=(f"ngm{par}",), writes=(f"ngm{par}",), waits=[t_rmx])
                tok_exp = P.op("scalar", lambda e, b=par: e.activation(
                    P_bf[b][:, :], S_sb[:, :], AF.Exp, bias=ngm[b][:, 0:1],
                    scale=1.0 / 32.0, accum_out=rsc[b][:, 0:1]),
                    reads=("S_sb",), writes=(f"P_bf{par}", f"rsc{par}"),
                    waits=[t_bias, tok_ptr_par[par], tok_riv[par]])
                tok_exp_par[par] = tok_exp
                if prev is not None:
                    tail_pv(prev)
                t_riv = P.op("vector", lambda e, b=par: e.reciprocal(
                    riv[b][:, 0:1], rsc[b][:, 0:1]),
                    reads=(f"rsc{par}",), writes=(f"riv{par}",),
                    waits=[tok_exp, tok_opv[par]])
                tok_riv[par] = t_riv
                prev = dict(par=par, t=t, exp=tok_exp, riv=t_riv, dp=dp)
            tail_tr(prev)
            tail_pv(prev)
            P.op("gpsimd", lambda e, v0=P.cnt["dG0"], v1=P.cnt["dG1"]: (
                e.wait_ge(P.sems["dG0"], v0), e.wait_ge(P.sems["dG1"], v1))[0],
                 inc=None)
            P.flush()
    return nc


def prep_inputs(inputs):
    d = inputs
    f32 = np.float32
    x = np.ascontiguousarray(np.asarray(d["L0_attn_x"], f32))
    com = {}
    com["wi"] = np.ascontiguousarray(np.asarray(d["L0_attn_Wi"], f32))
    com["wf"] = np.ascontiguousarray(np.asarray(d["L0_attn_Wf"], f32))
    com["wg"] = np.ascontiguousarray(np.asarray(d["L0_attn_Wg"], f32))
    com["wo"] = np.ascontiguousarray(np.asarray(d["L0_attn_Wo"], f32))
    com["ci"] = np.ascontiguousarray(
        np.asarray(d["L0_attn_conv_i"], f32).reshape(8, 128, 4)
        .transpose(1, 0, 2).reshape(128, 32))
    com["cf"] = np.ascontiguousarray(
        np.asarray(d["L0_attn_conv_f"], f32).reshape(8, 128, 4)
        .transpose(1, 0, 2).reshape(128, 32))
    com["gnw"] = np.ascontiguousarray(
        np.asarray(d["L0_attn_g_norm_weight"], f32).reshape(8, 128).T)
    for f in ("q", "k", "v"):
        com[f + "w1"] = np.ascontiguousarray(
            np.asarray(d[f"L0_ffn_{f}_w1"], f32).transpose(1, 0, 2).reshape(H, H))
        com[f + "b1"] = np.ascontiguousarray(
            np.asarray(d[f"L0_ffn_{f}_b1"], f32).reshape(8, 128).T)
        com[f + "w2"] = np.ascontiguousarray(
            np.asarray(d[f"L0_ffn_{f}_w2"], f32).reshape(H, H))
        b2p = np.zeros((128, H), f32)
        b2p[0:16] = np.asarray(d[f"L0_ffn_{f}_b2"], f32)
        com[f + "b2"] = b2p
        com[f + "nwT"] = np.ascontiguousarray(np.asarray(d[f"L0_ffn_{f}_node_w"], f32).T)
        com[f + "nb"] = np.ascontiguousarray(
            np.asarray(d[f"L0_ffn_{f}_node_b"], f32).reshape(1, 15))
    com["ident"] = np.eye(128, dtype=BF)
    ep = np.zeros((128, H), dtype=BF)
    for p in range(8):
        ep[2 * p, p * 128:p * 128 + 64] = 1
        ep[2 * p + 1, p * 128 + 64:p * 128 + 128] = 1
    com["epairs"] = ep
    com["iota16"] = np.tile(np.arange(16, dtype=f32), (128, 1))
    com["onesrow"] = np.ones((1, 128), dtype=BF)
    maps = []
    for c in range(NCORE):
        m = dict(com)
        m["x"] = np.ascontiguousarray(x[c])
        maps.append(m)
    return maps


def kernel(**inputs):
    maps = prep_inputs(inputs)
    nc = build_program()
    res = run_bass_kernel_spmd(nc, maps, core_ids=list(range(NCORE)))
    return np.stack([r["out"] for r in res.results], axis=0)


# version 14
# speedup vs baseline: 1.1836x; 1.1836x over previous
import numpy as np
import ml_dtypes
import concourse.bass as bass
import concourse.mybir as mybir
from concourse.bass_utils import run_bass_kernel_spmd
from contextlib import ExitStack

BF = ml_dtypes.bfloat16
F32 = mybir.dt.float32
BF16 = mybir.dt.bfloat16
OP = mybir.AluOpType
AF = mybir.ActivationFunctionType
AX = mybir.AxisListType.X

T = 4096
H = 1024
NS = 8
NT = 32
NCH = 8
NCORE = 8
ENG2SEM = {"tensor": "sT", "vector": "sV", "scalar": "sS"}


DMA_SEMS = ("dA0", "dA1", "dA2", "dU0", "dU1", "dP0", "dP1", "dW0", "dW1",
            "dN", "dNB", "dB1", "dB2", "dK", "dG0", "dG1", "dDBG",
            "dC0", "dC1", "dC2", "dC3", "dC4", "dC5", "dC6", "dC7", "dW2")


class Prog:
    ENGINES = ("sync", "gpsimd", "tensor", "vector", "scalar")

    def __init__(self, nc):
        self.nc = nc
        self.es = ExitStack()
        self.sems = {}
        self.cnt = {}
        for name in ("sT", "sV", "sS") + DMA_SEMS:
            self.sems[name] = self.es.enter_context(nc.semaphore(name))
            self.cnt[name] = 0
        self.streams = {e: [] for e in self.ENGINES}
        self.waited = {e: {} for e in self.ENGINES}
        self.dirtyW = {e: set() for e in self.ENGINES}
        self.dirtyR = {e: set() for e in self.ENGINES}

    def op(self, eng, fn, reads=(), writes=(), waits=(), inc="auto"):
        drain = False
        if eng in ("vector", "scalar", "gpsimd"):
            if (set(reads) & self.dirtyW[eng]) or (set(writes) & self.dirtyW[eng]) \
               or (set(writes) & self.dirtyR[eng]):
                drain = True
                self.dirtyW[eng] = set()
                self.dirtyR[eng] = set()
            self.dirtyW[eng] |= set(writes)
            self.dirtyR[eng] |= set(reads)
        if inc == "auto":
            inc = ENG2SEM[eng] if eng in ("vector", "scalar") else None
        tok = None
        amt = 1
        if inc is not None:
            self.cnt[inc] += amt
            tok = (inc, self.cnt[inc])
        own = ENG2SEM.get(eng)
        w = tuple((s, v) for (s, v) in [x for x in waits if x is not None]
                  if s != own)
        self.streams[eng].append((w, drain, fn, (inc, amt) if inc else None))
        return tok

    def dma(self, queue, dst, src, sem, waits=()):
        self.cnt[sem] += 16
        tok = (sem, self.cnt[sem])
        w = tuple((s, v) for (s, v) in [x for x in waits if x is not None])
        self.streams[queue].append(
            (w, False, lambda e, d=dst, s_=src: e.dma_start(d, s_), (sem, 16)))
        return tok

    def flush(self):
        with self.nc.Block(no_gpsimd_drain=True) as blk:
            for eng in self.ENGINES:
                ops = self.streams[eng]
                if not ops:
                    continue

                def make(ops_, eng_):
                    def emit(engine):
                        w8 = self.waited[eng_]
                        for waits, drain, fn, inc in ops_:
                            for (sn, val) in waits:
                                if w8.get(sn, 0) >= val:
                                    continue
                                engine.wait_ge(self.sems[sn], val)
                                w8[sn] = val
                            if drain:
                                engine.drain()
                            inst = fn(engine)
                            if inc is not None:
                                inst.then_inc(self.sems[inc[0]], inc[1])
                    return emit

                getattr(blk, eng)(make(list(ops), eng))
                self.streams[eng] = []
        for e in self.ENGINES:
            for s in ("sT", "sV", "sS"):
                self.waited[e][s] = max(self.waited[e].get(s, 0), self.cnt[s])
            self.dirtyW[e] = set()
            self.dirtyR[e] = set()


def ap3(handle, offset, dims):
    return bass.AP(handle, offset, [list(d) for d in dims])


def build_program(upto="D", dbg=()):
    nc = bass.Bass("TRN2", target_bir_lowering=False)
    DR = lambda name, shape, dt, kind="ExternalInput": nc.dram_tensor(
        name, shape, dt, kind=kind)

    x_h = DR("x", [T, H], F32)
    g_h = {k: DR(k, [H, H], F32) for k in ("wi", "wf", "wg", "wo")}
    ci_h = DR("ci", [128, 32], F32)
    cf_h = DR("cf", [128, 32], F32)
    gnw_h = DR("gnw", [128, 8], F32)
    fff_h = {}
    for f in ("q", "k", "v"):
        fff_h[f] = dict(
            w1=DR(f + "w1", [H, H], F32), b1=DR(f + "b1", [128, 8], F32),
            w2=DR(f + "w2", [H, H], F32), b2=DR(f + "b2", [128, H], F32),
            nwT=DR(f + "nwT", [H, 16], F32), nb=DR(f + "nb", [1, 16], F32))
    ident_h = DR("ident", [128, 128], BF16)
    epairs_h = DR("epairs", [128, H], BF16)
    iota16_h = DR("iota16", [128, 16], F32)
    onesrow_h = DR("onesrow", [1, 128], BF16)

    qkvT_h = [DR(n, [H, T], BF16, "Internal") for n in ("qTs", "kTs", "vTs")]
    wbf_h = {n: DR(n + "_bf", [H, H], BF16, "Internal")
             for n in ("wi", "wf", "wg", "wo", "qw1", "qw2", "kw1", "kw2",
                       "vw1", "vw2")}
    usp_h = DR("usp", [H, T], BF16, "Internal")
    out_h = DR("out", [T, H], F32, "ExternalOutput")
    dbg_h = {}
    if "xnT" in dbg:
        dbg_h["xnT"] = DR("dbg_xnT", [128, NS * T], BF16, "ExternalOutput")
    if "qkv" in dbg:
        for i, f in enumerate(("q", "k", "v")):
            dbg_h["qT_" + f] = DR("dbg_qT_" + f, [H, T], BF16, "ExternalOutput")
    if "u" in dbg:
        dbg_h["usp"] = DR("dbg_usp", [H, T], BF16, "ExternalOutput")
        dbg_h["rms"] = DR("dbg_rms", [128, 32], F32, "ExternalOutput")

    es = ExitStack()
    S_ = lambda name, shape, dt: es.enter_context(nc.sbuf_tensor(name, shape, dt))
    P_ = lambda name, shape, dt: es.enter_context(nc.psum_tensor(name, shape, dt))
    with es:
        A_xn = S_("A_xn", [128, NS * T], BF16)
        ident = S_("identS", [128, 128], BF16)
        epairs = S_("epairsS", [128, H], BF16)
        iota16 = S_("iota16S", [128, 16], F32)
        onesrow = S_("onesrowS", [1, 128], BF16)
        onescol = S_("onescolS", [128, 1], F32)
        epscol = S_("epscolS", [128, 1], F32)
        rms_all = S_("rms_all", [128, 32], F32)
        pA = P_("pA", [128, 512], F32)
        pB = P_("pB", [128, 512], F32)
        pC = P_("pC", [128, 512], F32)
        pD = P_("pD", [128, 512], F32)
        pS = P_("pS", [128, 512], F32)
        pT0 = P_("pT0", [128, 1024], BF16)
        pT1 = P_("pT1", [128, 1024], BF16)
        fb = [pA, pB, pC, pD]
        pTb = [pT0, pT1]

        P = Prog(nc)

        P.dma("sync", ident[:, :], ident_h.ap(), "dK")
        P.dma("sync", epairs[:, :], epairs_h.ap(), "dK")
        P.dma("sync", iota16[:, :], iota16_h.ap(), "dK")
        P.dma("sync", onesrow[:, :], onesrow_h.ap(), "dK")
        tok_K = ("dK", P.cnt["dK"])
        P.op("vector", lambda e: e.memset(onescol[:, :], 1.0), writes=("onescol",))
        P.op("vector", lambda e: e.memset(epscol[:, :], 1e-5), writes=("epscol",))

        esA = ExitStack()
        with esA:
            SA = lambda n, sh, dt: esA.enter_context(nc.sbuf_tensor(n, sh, dt))
            xt = [SA(f"xt{i}", [128, H], F32) for i in range(3)]
            dump = SA("dumpA", [128, H], F32)
            xnb = [SA(f"xnb{i}", [128, H], BF16) for i in range(2)]
            cS = [SA(f"cS{i}", [128, 1], F32) for i in range(2)]
            cQ = [SA(f"cQ{i}", [128, 1], F32) for i in range(2)]
            cNM = [SA(f"cNM{i}", [128, 1], F32) for i in range(2)]
            cM2 = [SA(f"cM2{i}", [128, 1], F32) for i in range(2)]
            cV = [SA(f"cV{i}", [128, 1], F32) for i in range(2)]
            cR = [SA(f"cR{i}", [128, 1], F32) for i in range(2)]
            tok_tr = [None, None]
            tok_ev = [None, None]
            tok_nrm = [None, None]
            tok_nrm3 = [None, None, None]
            prevA = None

            def tailA(pv):
                b = pv["par"]
                xi_ = pv["xi"]
                t = pv["t"]
                P.op("vector", lambda e, b_=b: e.reciprocal(cR[b_][:, 0:1], cR[b_][:, 0:1]),
                     reads=(f"cR{b}",), writes=(f"cR{b}",), waits=[pv["sqrt"]])
                tnorm = P.op("vector", lambda e, b_=b, x_=xi_: e.tensor_scalar(
                    xnb[b_][:, :], xt[x_][:, :], cNM[b_][:, 0:1], cR[b_][:, 0:1],
                    OP.add, OP.mult),
                    reads=(f"xt{xi_}", f"cNM{b}", f"cR{b}"), writes=(f"xnb{b}",),
                    waits=[tok_tr[b]])
                tok_nrm[b] = tnorm
                tok_nrm3[xi_] = tnorm
                tk = None
                for s in range(NS):
                    tk = P.op("tensor", lambda e, b_=b, s_=s: e.transpose(
                        pTb[b_][:, s_ * 128:(s_ + 1) * 128],
                        xnb[b_][:, s_ * 128:(s_ + 1) * 128], ident[:, :]),
                        waits=([tnorm, tok_ev[b], tok_K] if s == 0 else []),
                        inc=("sT" if s == NS - 1 else None))
                tok_tr[b] = tk
                dst = ap3(A_xn, t * 128, [(NS * T, 128), (T, 8), (1, 128)])
                tok_ev[b] = P.op("scalar", lambda e, b_=b, d=dst: e.copy(
                    d, pTb[b_][:, :].rearrange("p (b c) -> p b c", c=128)),
                    writes=("A_xn",), waits=[tk])

            dmas = {}
            XSEM = ("dA0", "dA1", "dA2")
            for t0 in range(2):
                dmas[t0] = P.dma("sync", xt[t0 % 3][:, :],
                                 x_h.ap()[t0 * 128:(t0 + 1) * 128, :],
                                 XSEM[t0 % 3], waits=[])
            for t in range(NT):
                bi = t % 2
                xi = t % 3
                if prevA is not None:
                    tailA(prevA)
                if t + 2 < NT:
                    tp2 = t + 2
                    dmas[tp2] = P.dma("sync", xt[tp2 % 3][:, :],
                                      x_h.ap()[tp2 * 128:(tp2 + 1) * 128, :],
                                      XSEM[tp2 % 3],
                                      waits=[tok_nrm3[tp2 % 3]])
                dq = dmas[t]
                xb = f"xt{xi}"
                P.op("vector", lambda e, b=xi, c=bi: e.reduce_sum(
                    cS[c][:, 0:1], xt[b][:, :], axis=AX),
                    reads=(xb,), writes=(f"cS{bi}",), waits=[dq])
                tsqx = P.op("scalar", lambda e, b=xi, c=bi: e.activation(
                    dump[:, :], xt[b][:, :], AF.Square,
                    accum_out=cQ[c][:, 0:1]),
                    reads=(xb,), writes=("dumpA", f"cQ{bi}"), waits=[dq])
                P.op("vector", lambda e, b=bi: e.tensor_scalar(
                    cNM[b][:, 0:1], cS[b][:, 0:1], -1.0 / H, None, OP.mult),
                    reads=(f"cS{bi}",), writes=(f"cNM{bi}",))
                P.op("vector", lambda e, b=bi: e.tensor_tensor(
                    cM2[b][:, 0:1], cNM[b][:, 0:1], cNM[b][:, 0:1], OP.mult),
                    reads=(f"cNM{bi}",), writes=(f"cM2{bi}",))
                tva = P.op("vector", lambda e, b=bi: e.scalar_tensor_tensor(
                    cV[b][:, 0:1], cQ[b][:, 0:1], 1.0 / H, cM2[b][:, 0:1],
                    OP.mult, OP.subtract),
                    reads=(f"cQ{bi}", f"cM2{bi}"), writes=(f"cV{bi}",),
                    waits=[tsqx])
                tsq = P.op("scalar", lambda e, b=bi: e.activation(
                    cR[b][:, 0:1], cV[b][:, 0:1], AF.Sqrt, bias=epscol[:, 0:1]),
                    reads=(f"cV{bi}", "epscolS"), writes=(f"cR{bi}",),
                    waits=[tva, tok_nrm[bi]])
                prevA = dict(par=bi, xi=xi, t=t, sqrt=tsq)
            tailA(prevA)
            if "xnT" in dbg:
                P.dma("gpsimd", dbg_h["xnT"].ap(), A_xn[:, :], "dDBG",
                      waits=[tok_ev[0], tok_ev[1]])
            P.flush()

        if upto == "A":
            return nc

        tok_qkv_store = None
        esB = ExitStack()
        with esB:
            SB = lambda n, sh, dt: esB.enter_context(nc.sbuf_tensor(n, sh, dt))
            A_big = SB("A_big", [128, NS * T], BF16)
            selT = SB("selT", [128, T], BF16)
            S_all = SB("S_allB", [128, 512], F32)
            wld = [SB(f"wldB{i}", [128, H], BF16) for i in range(2)]
            w2ld = [SB(f"w2ldB{i}", [128, H], BF16) for i in range(2)]
            cwl = [SB(f"cwl{i}", [128, H], F32) for i in range(4)]
            cwb = [SB(f"cwb{i}", [128, H], BF16) for i in range(4)]
            cast_q = []
            b1c = SB("b1c", [128, 8], F32)
            b2bf = SB("b2bf", [128, H], BF16)
            b2ld = SB("b2ld", [128, H], F32)
            nwbf = SB("nwbf", [128, 128], BF16)
            nwld = SB("nwld", [128, 128], F32)
            nbbf = SB("nbbf", [1, 16], BF16)
            nbld = SB("nbld", [1, 16], F32)
            Hf = [SB(f"Hf{i}", [128, 512], F32) for i in range(5)]
            qst = [SB(f"qst{i}", [128, T], BF16) for i in range(2)]
            tr1 = SB("tr1", [128, 32], F32)
            tr2 = SB("tr2", [128, 32], F32)
            tr3 = SB("tr3", [128, 32], F32)
            bb0 = SB("bb0", [128, 32], F32)
            bb1 = SB("bb1", [128, 32], F32)
            bb2 = SB("bb2", [128, 32], F32)
            leaf = SB("leafB", [128, 32], F32)
            ohb = SB("ohb", [128, 16], BF16)

            tz_sel = P.op("vector", lambda e: e.memset(selT[:, :], 0.0),
                          writes=("selT",))
            cast_st = {}
            CLSEM = ("dC0", "dC1", "dC2", "dC3")
            CSSEM = ("dC4", "dC5", "dC6", "dC7")
            cast_state = dict(ld=[None] * 4, cst=[None] * 4, stp=0)

            def cast_step():
                if not cast_q:
                    return
                name, s8 = cast_q.pop(0)
                src_h = g_h[name] if name in g_h else \
                    fff_h[name[0]]["w1" if name.endswith("1") else "w2"]
                i = cast_state["stp"]
                cast_state["stp"] += 1
                par = i % 4
                dl = P.dma("sync", cwl[par][:, :],
                           ap3(src_h, s8 * 128 * H, [(H, 128), (1, H)]),
                           CLSEM[par], waits=[cast_state["cst"][par]])
                eng = "vector" if i % 2 == 0 else "scalar"
                if eng == "vector":
                    ct = P.op("vector", lambda e, b=par: e.tensor_copy(
                        cwb[b][:, :], cwl[b][:, :]),
                        reads=(f"cwl{par}",), writes=(f"cwb{par}",),
                        waits=[dl, cast_state["ld"][par]])
                else:
                    ct = P.op("scalar", lambda e, b=par: e.copy(
                        cwb[b][:, :], cwl[b][:, :]),
                        reads=(f"cwl{par}",), writes=(f"cwb{par}",),
                        waits=[dl, cast_state["ld"][par]])
                cast_state["cst"][par] = ct
                last = P.dma("gpsimd",
                             ap3(wbf_h[name], s8 * 128 * H, [(H, 128), (1, H)]),
                             cwb[par][:, :], CSSEM[par], waits=[ct])
                cast_state["ld"][par] = last
                if s8 == NS - 1:
                    cast_st[name] = tuple((sm, P.cnt[sm]) for sm in CSSEM)

            def queue_wcast(name):
                for s8 in range(NS):
                    cast_q.append((name, s8))
                cast_st.pop(name, None)

            def drain_wcast(name):
                while name not in cast_st:
                    cast_step()

            queue_wcast("qw1")
            queue_wcast("qw2")
            drain_wcast("qw1")
            drain_wcast("qw2")
            tok_qst = [None, None]
            tok_qev = [None, None]
            tok_w1c = [None, None]
            tok_w1te = [None, None]
            tok_w2te = [None, None]
            tok_hev = [None, None]
            tok_mmul = [None] * 5
            tok_hbank = [None, None]
            tok_kbank = [None, None]
            tok_tev_prev = None
            tok_tlf_prev = None
            tok_scmm_prev = None
            tok_w2mm_prev = None
            c1 = c2 = c3 = None
            for fi, f in enumerate(("q", "k", "v")):
                hh = fff_h[f]
                dst_h = qkvT_h[fi]
                nw_src = ap3(hh["nwT"], 0, [(16, 128), (16 * 128, 8), (1, 16)])
                t_nw = P.dma("sync", nwld[:, :].rearrange("p (b c) -> p b c", c=16),
                             nw_src, "dN", waits=[c1])
                t_nb = P.dma("sync", nbld[:, :], hh["nb"].ap(), "dNB", waits=[c2])
                t_b1 = P.dma("sync", b1c[:, :], hh["b1"].ap(), "dB1",
                             waits=[tok_hev[0], tok_hev[1]])
                t_b2 = P.dma("sync", b2ld[:, :], hh["b2"].ap(), "dB2", waits=[c3])
                c1 = P.op("vector", lambda e: e.tensor_copy(nwbf[:, :], nwld[:, :]),
                          reads=("nwld",), writes=("nwbf",),
                          waits=[t_nw, tok_scmm_prev])
                c2 = P.op("vector", lambda e: e.tensor_copy(nbbf[:, :], nbld[:, :]),
                          reads=("nbld",), writes=("nbbf",),
                          waits=[t_nb, tok_scmm_prev])
                c3 = P.op("vector", lambda e: e.tensor_copy(b2bf[:, :], b2ld[:, :]),
                          reads=("b2ld",), writes=("b2bf",),
                          waits=[t_b2, tok_w2mm_prev])
                tok_sc = None
                for t in range(NT):
                    for s in range(NS):
                        P.op("tensor", lambda e, t_=t, s_=s: e.matmul(
                            pS[:, 16 * t_:16 * t_ + 16],
                            A_xn[:, s_ * T + t_ * 128:s_ * T + t_ * 128 + 128],
                            nwbf[:, s_ * 16:s_ * 16 + 16],
                            start=(s_ == 0), stop=False),
                            waits=([c1, c2, tok_K, tok_tev_prev]
                                   if (t == 0 and s == 0) else []),
                            inc=None)
                    tok_sc = P.op("tensor", lambda e, t_=t: e.matmul(
                        pS[:, 16 * t_:16 * t_ + 16], onesrow[:, :], nbbf[:, :],
                        start=False, stop=True),
                        inc=("sT" if t == NT - 1 else None))
                tok_scmm_prev = tok_sc
                tev = P.op("scalar", lambda e: e.copy(S_all[:, :], pS[:, :]),
                           writes=("S_allB",), waits=[tok_sc, tok_tlf_prev])
                tok_tev_prev = tev
                col = lambda j: ap3(S_all, j, [(512, 128), (16, 32)])
                V = "vector"

                def tt(d, a, b_, o, extra_w=()):
                    return P.op(V, lambda e: e.tensor_tensor(d[1], a[1], b_[1], o),
                                reads=(a[0], b_[0]), writes=(d[0],), waits=extra_w)

                def ge(d, a, extra_w=()):
                    return P.op(V, lambda e: e.tensor_scalar(
                        d[1], a[1], 0.0, None, OP.is_ge),
                        reads=(a[0],), writes=(d[0],), waits=extra_w)

                SA_ = lambda j: ("S_allB", col(j))
                TR1 = ("tr1", tr1[:, :]); TR2 = ("tr2", tr2[:, :])
                TR3 = ("tr3", tr3[:, :])
                B0 = ("bb0", bb0[:, :]); B1 = ("bb1", bb1[:, :])
                B2 = ("bb2", bb2[:, :])
                ge(B0, SA_(0), extra_w=[tev])

                def lerp(dst, lo, hi, bit):
                    tt(dst, hi, lo, OP.subtract)
                    tt(dst, dst, bit, OP.mult)
                    tt(dst, dst, lo, OP.add)
                lerp(TR1, SA_(1), SA_(2), B0)
                ge(B1, TR1)
                lerp(TR1, SA_(3), SA_(4), B1)
                lerp(TR2, SA_(5), SA_(6), B1)
                lerp(TR2, TR1, TR2, B0)
                ge(B2, TR2)
                lerp(TR1, SA_(7), SA_(8), B2)
                lerp(TR2, SA_(9), SA_(10), B2)
                lerp(TR2, TR1, TR2, B1)
                lerp(TR1, SA_(11), SA_(12), B2)
                lerp(TR3, SA_(13), SA_(14), B2)
                lerp(TR3, TR1, TR3, B1)
                lerp(TR3, TR2, TR3, B0)
                ge(TR3, TR3)
                P.op(V, lambda e: e.scalar_tensor_tensor(
                    leaf[:, :], bb0[:, :], 2.0, bb1[:, :], OP.mult, OP.add),
                    reads=("bb0", "bb1"), writes=("leafB",))
                P.op(V, lambda e: e.scalar_tensor_tensor(
                    leaf[:, :], leaf[:, :], 2.0, bb2[:, :], OP.mult, OP.add),
                    reads=("leafB", "bb2"), writes=("leafB",))
                tlf = P.op(V, lambda e: e.scalar_tensor_tensor(
                    leaf[:, :], leaf[:, :], 2.0, tr3[:, :], OP.mult, OP.add),
                    reads=("leafB", "tr3"), writes=("leafB",))
                tok_tlf_prev = tlf
                last_h = None
                w1name = f + "w1"
                drain_wcast(w1name)
                tok_hev_slot = {}
                t_h_p0 = [None]

                def w1_load(p_):
                    bi_ = p_ % 2
                    wsrc = ap3(wbf_h[w1name], p_ * 128,
                               [(H, 128), (128 * H, 8), (1, 128)])
                    tok_w1c[bi_] = P.dma(
                        "sync", wld[bi_][:, :].rearrange("p (b c) -> p b c", c=128),
                        wsrc, "dW0" if bi_ == 0 else "dW1",
                        waits=[tok_w1te[bi_], tok_w2te[bi_]] + list(cast_st[w1name]))

                def h_group(p_, ch_, hslot):
                    bi_ = p_ % 2
                    par = (p_ * NCH + ch_) % 2
                    mb = fb[par]
                    tkh = None
                    for s in range(NS):
                        tkh = P.op("tensor", lambda e, s_=s, c_=ch_, b=bi_, m=mb: e.matmul(
                            m[:, :], wld[b][:, s_ * 128:(s_ + 1) * 128],
                            A_xn[:, s_ * T + c_ * 512:s_ * T + c_ * 512 + 512],
                            start=(s_ == 0), stop=(s_ == NS - 1)),
                            waits=([tok_w1c[bi_], tok_hbank[par], tok_qev[par]]
                                   if s == 0 else []),
                            inc=("sT" if s == NS - 1 else None))
                    t_he = P.op("scalar", lambda e, q_=p_, hs=hslot, m=mb: e.activation(
                        Hf[hs][:, :], m[:, :], AF.Relu, bias=b1c[:, q_:q_ + 1]),
                        reads=("b1c",), writes=(f"Hf{hslot}",),
                        waits=[tkh, t_b1, tok_mmul[hslot]])
                    tok_hev[par] = t_he
                    tok_hbank[par] = t_he
                    tok_hev_slot[hslot] = t_he
                    return tkh

                def m_group(p_, ch_, hslot):
                    par = (p_ * NCH + ch_) % 2
                    kb = fb[2 + par]
                    t_m = P.op("tensor", lambda e, q_=p_, c_=ch_, k=kb: e.matmul(
                        k[:, :], epairs[:, q_ * 128:(q_ + 1) * 128],
                        selT[:, c_ * 512:c_ * 512 + 512],
                        start=True, stop=True),
                        waits=[tok_sel[0], tok_kbank[par]], inc="sT")
                    t_he = tok_hev_slot[hslot]
                    t_mm = P.op(V, lambda e, q_=p_, c_=ch_, hs=hslot, k=kb: e.tensor_tensor(
                        A_big[:, q_ * T + c_ * 512:q_ * T + c_ * 512 + 512],
                        Hf[hs][:, :], k[:, :], OP.mult),
                        reads=(f"Hf{hslot}",), writes=("A_big",),
                        waits=[t_he, t_m])
                    tok_mmul[hslot] = t_mm
                    tok_kbank[par] = t_mm
                    return t_mm

                tok_sel = [None]
                L = 5
                w1_load(0)
                for ch in range(L):
                    t_h_p0[0] = h_group(0, ch, ch % L)
                tok_ohT = None
                tok_selev = [None, None]
                first_sel = True
                for t in range(NT):
                    gi = (t // 8) % 2
                    toh = P.op(V, lambda e, t_=t: e.tensor_scalar(
                        ohb[:, :], iota16[:, :], leaf[:, t_:t_ + 1], None,
                        OP.is_equal),
                        reads=("iota16S", "leafB"), writes=("ohb",),
                        waits=[tlf, tok_ohT])
                    tok_ohT = P.op("tensor", lambda e, t_=t, g=gi: e.transpose(
                        pTb[g][0:16, (t_ % 8) * 128:(t_ % 8) * 128 + 128],
                        ohb[:, :], ident[:, :]),
                        waits=[toh] + ([tok_selev[gi]] if t % 8 == 0 else []),
                        inc="sT")
                    if t % 8 == 7:
                        t0 = t - 7
                        w_extra = [tok_w2mm_prev, tz_sel] if first_sel else []
                        first_sel = False
                        tok_selev[gi] = P.op(V, lambda e, t0_=t0, g=gi: e.tensor_copy(
                            selT[0:16, t0_ * 128:(t0_ + 8) * 128], pTb[g][0:16, :]),
                            writes=("selT",), waits=[tok_ohT] + w_extra)
                tok_sel[0] = tok_selev[1]
                for wname in (("kw1", "kw2", "wi") if fi == 0 else
                              ("vw1", "vw2", "wf") if fi == 1 else
                              ("wg", "wo")):
                    queue_wcast(wname)
                for ch in range(L, NCH):
                    m_group(0, ch - L, (ch - L) % L)
                    t_h_p0[0] = h_group(0, ch, ch % L)
                for ch in range(NCH - L, NCH):
                    last_h = m_group(0, ch, ch % L)
                tok_w1te[0] = t_h_p0[0]
                for p in range(1, NS):
                    bi = p % 2
                    cast_step()
                    cast_step()
                    w1_load(p)
                    t_h = None
                    for ch in range(NCH):
                        t_h = h_group(p, ch, ch % 2)
                        last_h = m_group(p, ch, ch % 2)
                    tok_w1te[bi] = t_h
                tq = None
                w2name = f + "w2"
                drain_wcast(w2name)
                for dch in range(NS):
                    bi = dch % 2
                    cast_step()
                    wsrc = ap3(wbf_h[w2name], dch * 128,
                               [(H, 128), (128 * H, 8), (1, 128)])
                    tok_w2c2 = P.dma(
                        "sync", w2ld[bi][:, :].rearrange("p (b c) -> p b c", c=128),
                        wsrc, "dW0" if bi == 0 else "dW1",
                        waits=[tok_w2te[bi], tok_w1c[bi]] + list(cast_st[w2name]))
                    sp = dch % 2
                    for ch in range(NCH):
                        par = (dch * NCH + ch) % 2
                        mb = fb[par]
                        for p in range(NS):
                            P.op("tensor", lambda e, p_=p, ch_=ch, b=bi, m=mb: e.matmul(
                                m[:, :], w2ld[b][:, p_ * 128:(p_ + 1) * 128],
                                A_big[:, p_ * T + ch_ * 512:p_ * T + ch_ * 512 + 512],
                                start=(p_ == 0), stop=False),
                                waits=([tok_w2c2, last_h, c3, tok_qev[par]]
                                       if p == 0 else []), inc=None)
                        tq = P.op("tensor", lambda e, ch_=ch, d_=dch, m=mb: e.matmul(
                            m[:, :], b2bf[:, d_ * 128:d_ * 128 + 128],
                            selT[:, ch_ * 512:ch_ * 512 + 512],
                            start=False, stop=True), inc="sT")
                        tok_qev[par] = P.op("scalar", lambda e, ch_=ch, s=sp, m=mb: e.copy(
                            qst[s][:, ch_ * 512:ch_ * 512 + 512], m[:, :]),
                            writes=(f"qst{sp}",),
                            waits=[tq] + ([tok_qst[sp]] if ch == 0 else []))
                    tok_w2te[bi] = tq
                    tok_qst[sp] = P.dma(
                        "gpsimd", ap3(dst_h, dch * 128 * T, [(T, 128), (1, T)]),
                        qst[sp][:, :], "dG0" if sp == 0 else "dG1",
                        waits=[tok_qev[0], tok_qev[1]])
                tok_w2mm_prev = tq
                if "qkv" in dbg:
                    P.dma("gpsimd", dbg_h["qT_" + f].ap(), dst_h.ap(), "dDBG",
                          waits=[tok_qst[0], tok_qst[1]])
            while cast_q:
                cast_step()
            P.flush()
        tok_B_st0 = ("dG0", P.cnt["dG0"])
        tok_B_st1 = ("dG1", P.cnt["dG1"])

        if upto == "B":
            return nc

        esC = ExitStack()
        with esC:
            SC = lambda n, sh, dt: esC.enter_context(nc.sbuf_tensor(n, sh, dt))
            C1 = SC("C1", [128, T + 4], F32)
            C2 = SC("C2", [128, T + 4], F32)
            C3 = SC("C3", [128, T], F32)
            C4 = SC("C4", [128, T], F32)
            C5 = SC("C5", [128, T], F32)
            sqb = SC("sqb", [128, T], BF16)
            wldC = [SC(f"wldC{i}", [128, H], BF16) for i in range(3)]
            cicol = SC("cicol", [128, 32], F32)
            cfcol = SC("cfcol", [128, 32], F32)
            gnwc = SC("gnwc", [128, 8], F32)
            ssq = SC("ssq", [128, 32], F32)
            onescol_bf = SC("onescol_bf", [128, 1], BF16)
            ust = [SC(f"ust{i}", [128, T], BF16) for i in range(2)]

            P.dma("sync", cicol[:, :], ci_h.ap(), "dK")
            P.dma("sync", cfcol[:, :], cf_h.ap(), "dK")
            P.dma("sync", gnwc[:, :], gnw_h.ap(), "dK")
            tok_KC = ("dK", P.cnt["dK"])
            d_ci = d_cf = d_gn = tok_KC
            tz1 = P.op("vector", lambda e: e.memset(C1[:, 0:3], 0.0), writes=("C1",))
            tz2 = P.op("vector", lambda e: e.memset(C2[:, 0:3], 0.0), writes=("C2",))
            tz3 = P.op("vector", lambda e: e.memset(ssq[:, :], 0.0), writes=("ssq",))
            tzo = P.op("vector", lambda e: e.memset(onescol_bf[:, :], 1.0),
                       writes=("onescol_bf",))

            gates = [("wi", C1, 3, "C1"), ("wf", C2, 3, "C2"), ("wg", C3, 0, "C3")]
            st_c = dict(wload=[None, None, None], wte=[None, None, None],
                        bank=[None, None, None, None],
                        convi=None, convf=None, u=None, scan=None, sq=None,
                        ssqmm=None, udma=[None, None], ssqacc=None)
            evs = {}

            WSEM = ("dW0", "dW1", "dW2")

            def emit_wload(s, gidx):
                gname = gates[gidx][0]
                par = (s * 3 + gidx) % 3
                wsrc = ap3(wbf_h[gname], s * 128, [(H, 128), (128 * H, 8), (1, 128)])
                st_c["wload"][par] = P.dma(
                    "sync", wldC[par][:, :].rearrange("p (b c) -> p b c", c=128),
                    wsrc, WSEM[par],
                    waits=[st_c["wte"][par]] + list(cast_st[gname]))

            def emit_group(s, gidx, ch):
                par = (s * 3 + gidx) % 3
                bpar = (gidx * NCH + ch) % 4
                mb = fb[bpar]
                dstbuf, ofs, bufname = gates[gidx][1], gates[gidx][2], gates[gidx][3]
                tg = None
                for hs in range(NS):
                    tg = P.op("tensor", lambda e, h_=hs, c_=ch, b=par, m=mb: e.matmul(
                        m[:, :], wldC[b][:, h_ * 128:(h_ + 1) * 128],
                        A_xn[:, h_ * T + c_ * 512:h_ * T + c_ * 512 + 512],
                        start=(h_ == 0), stop=(h_ == NS - 1)),
                        waits=([st_c["wload"][par], st_c["bank"][bpar]]
                               if hs == 0 else []),
                        inc=("sT" if hs == NS - 1 else None))
                st_c["wte"][par] = tg
                if gidx == 0:
                    dfree = [st_c["convi"], tz1]
                elif gidx == 1:
                    dfree = [st_c["convf"], tz2]
                else:
                    dfree = [st_c["u"]]
                tev = P.op("scalar", lambda e, d=dstbuf, o=ofs, c_=ch, m=mb: e.copy(
                    d[:, o + c_ * 512:o + c_ * 512 + 512], m[:, :]),
                    writes=(bufname,), waits=[tg] + dfree)
                st_c["bank"][bpar] = tev
                evs.setdefault(s, {}).setdefault(gidx, []).append(tev)

            def emit_ssq(s):
                tmm = None
                for t in range(NT):
                    tmm = P.op("tensor", lambda e, t_=t: e.matmul(
                        pS[:, t_:t_ + 1], sqb[:, t_ * 128:(t_ + 1) * 128],
                        onescol_bf[:, :], start=True, stop=True),
                        waits=([st_c["sq"], st_c["ssqacc"], tzo] if t == 0 else []),
                        inc=("sT" if t == NT - 1 else None))
                st_c["ssqmm"] = tmm

            for gidx in range(3):
                emit_wload(0, gidx)
                for ch in range(NCH):
                    emit_group(0, gidx, ch)

            for s in range(NS):
                if s + 1 < NS:
                    for gidx in range(3):
                        emit_wload(s + 1, gidx)
                if s >= 1:
                    emit_ssq(s - 1)
                    st_c["ssqacc"] = P.op("vector", lambda e: e.tensor_tensor(
                        ssq[:, :], ssq[:, :], pS[:, 0:32], OP.add),
                        reads=("ssq",), writes=("ssq",),
                        waits=[st_c["ssqmm"], tz3])
                ev_i = evs[s][0]
                ev_f = evs[s][1]
                ev_g = evs[s][2]
                w0 = ev_i + [d_ci]
                P.op("vector", lambda e, s_=s: e.tensor_scalar(
                    C4[:, :], C1[:, 0:T], cicol[:, 4 * s_:4 * s_ + 1], None, OP.mult),
                    reads=("C1", "cicol"), writes=("C4",), waits=w0)
                for k in range(1, 4):
                    P.op("vector", lambda e, s_=s, k_=k: e.scalar_tensor_tensor(
                        C4[:, :], C1[:, k_:k_ + T], cicol[:, 4 * s_ + k_:4 * s_ + k_ + 1],
                        C4[:, :], OP.mult, OP.add),
                        reads=("C1", "C4", "cicol"), writes=("C4",))
                st_c["convi"] = ("sV", P.cnt["sV"])
                w0 = ev_f + [d_cf, st_c["sq"]]
                P.op("vector", lambda e, s_=s: e.tensor_scalar(
                    C5[:, :], C2[:, 0:T], cfcol[:, 4 * s_:4 * s_ + 1], None, OP.mult),
                    reads=("C2", "cfcol"), writes=("C5",), waits=w0)
                for k in range(1, 4):
                    P.op("vector", lambda e, s_=s, k_=k: e.scalar_tensor_tensor(
                        C5[:, :], C2[:, k_:k_ + T], cfcol[:, 4 * s_ + k_:4 * s_ + k_ + 1],
                        C5[:, :], OP.mult, OP.add),
                        reads=("C2", "C5", "cfcol"), writes=("C5",))
                st_c["convf"] = ("sV", P.cnt["sV"])
                t_sg = P.op("scalar", lambda e: e.activation(
                    C3[:, :], C3[:, :], AF.Silu),
                    reads=("C3",), writes=("C3",), waits=ev_g)
                t_silu = P.op("scalar", lambda e: e.activation(
                    C4[:, :], C4[:, :], AF.Silu),
                    reads=("C4",), writes=("C4",), waits=[st_c["convi"]])
                if s + 1 < NS:
                    for ch in range(NCH):
                        emit_group(s + 1, 0, ch)
                t_sig = P.op("scalar", lambda e: e.activation(
                    C5[:, :], C5[:, :], AF.Sigmoid),
                    reads=("C5",), writes=("C5",), waits=[st_c["convf"]])
                if s + 1 < NS:
                    for ch in range(NCH):
                        emit_group(s + 1, 1, ch)
                t_ig = P.op("vector", lambda e: e.scalar_tensor_tensor(
                    C4[:, :], C5[:, :], 1.0, C4[:, :], OP.subtract, OP.mult),
                    reads=("C5", "C4"), writes=("C4",), waits=[t_sig, t_silu])
                st_c["scan"] = P.op("vector", lambda e: e.tensor_tensor_scan(
                    C5[:, :], C5[:, :], C4[:, :], 0.0, OP.mult, OP.subtract),
                    reads=("C5", "C4"), writes=("C5",), waits=[t_ig])
                st_c["sq"] = P.op("scalar", lambda e: e.activation(
                    sqb[:, :], C5[:, :], AF.Square),
                    reads=("C5",), writes=("sqb",),
                    waits=[st_c["scan"], st_c["ssqmm"]])
                sp = s % 2
                st_c["u"] = P.op("vector", lambda e, s_=s, p_=sp: e.scalar_tensor_tensor(
                    ust[p_][:, :], C5[:, :], gnwc[:, s_:s_ + 1], C3[:, :],
                    OP.mult, OP.mult),
                    reads=("C5", "C3", "gnwc"), writes=(f"ust{sp}",),
                    waits=[st_c["scan"], t_sg, d_gn, st_c["udma"][sp]])
                st_c["udma"][sp] = P.dma(
                    "gpsimd", ap3(usp_h, s * 128 * T, [(T, 128), (1, T)]),
                    ust[sp][:, :], "dG0" if sp == 0 else "dG1",
                    waits=[st_c["u"]])
                if s + 1 < NS:
                    for ch in range(NCH):
                        emit_group(s + 1, 2, ch)
            emit_ssq(NS - 1)
            st_c["ssqacc"] = P.op("vector", lambda e: e.tensor_tensor(
                ssq[:, :], ssq[:, :], pS[:, 0:32], OP.add),
                reads=("ssq",), writes=("ssq",), waits=[st_c["ssqmm"], tz3])
            t_rs = P.op("scalar", lambda e: e.activation(
                rms_all[:, :], ssq[:, :], AF.Sqrt, bias=epscol[:, 0:1],
                scale=1.0 / H),
                reads=("ssq", "epscolS"), writes=("rms_all",),
                waits=[st_c["ssqacc"]])
            P.op("vector", lambda e: e.reciprocal(rms_all[:, :], rms_all[:, :]),
                 reads=("rms_all",), writes=("rms_all",), waits=[t_rs])
            if "u" in dbg:
                P.dma("gpsimd", dbg_h["usp"].ap(), usp_h.ap(), "dDBG",
                      waits=[st_c["udma"][0], st_c["udma"][1]])
                P.dma("gpsimd", dbg_h["rms"].ap(), rms_all[:, :], "dDBG")
            P.flush()
        tok_C_st0 = ("dG0", P.cnt["dG0"])
        tok_C_st1 = ("dG1", P.cnt["dG1"])

        if upto == "C":
            return nc

        esD = ExitStack()
        with esD:
            SD = lambda n, sh, dt: esD.enter_context(nc.sbuf_tensor(n, sh, dt))
            A_big = SD("A_bigD", [128, NS * T], BF16)
            qb = [SD(f"qbD{i}", [128, H], BF16) for i in range(2)]
            ptl = [SD(f"ptl{i}", [128, H], F32) for i in range(2)]
            otD = [SD(f"otD{i}", [128, H], F32) for i in range(2)]
            mx8 = [SD(f"mx8{i}", [128, 8], F32) for i in range(2)]
            ngm = [SD(f"ngm{i}", [128, 1], F32) for i in range(2)]
            rsc = [SD(f"rsc{i}", [128, 1], F32) for i in range(2)]
            riv = [SD(f"riv{i}", [128, 1], F32) for i in range(2)]

            tok_addD = [None, None]
            tok_odmaD = [None, None]
            tok_sev = [None, None]
            esC2 = ExitStack()
            with esC2:
                SC2 = lambda n, sh, dt: esC2.enter_context(nc.sbuf_tensor(n, sh, dt))
                wo_bf = SC2("wo_sb", [128, NS * H], BF16)
                ub = [SC2(f"ub{i}", [128, H], BF16) for i in range(2)]
                vbD = SC2("vbD", [128, T], BF16)
                for s in range(NS):
                    P.dma("sync", wo_bf[:, s * H:(s + 1) * H],
                          ap3(wbf_h["wo"], s * 128 * H, [(H, 128), (1, H)]),
                          "dK", waits=list(cast_st["wo"]) if s == 0 else [])
                tok_wo = [("dK", P.cnt["dK"])]
                for s in range(NS):
                    P.dma("sync", A_xn[:, s * T:(s + 1) * T],
                          ap3(qkvT_h[1], s * 128 * T, [(T, 128), (1, T)]),
                          "dK", waits=[tok_B_st0, tok_B_st1])
                tok_KD = ("dK", P.cnt["dK"])
                tok_umm = [None, None]
                tok_oev = [None, None]
                tok_vev = [None, None]
                tok_vtr = None
                vstate = dict(ds=0, kt=0, dv=None)

                def v_step():
                    ds = vstate["ds"]
                    if ds >= NS:
                        return
                    kt0 = vstate["kt"]
                    if kt0 == 0:
                        vstate["dv"] = P.dma(
                            "sync", vbD[:, :],
                            ap3(qkvT_h[2], ds * 128 * T, [(T, 128), (1, T)]),
                            "dNB", waits=[vstate.get("prev_tr"), tok_B_st0, tok_B_st1])
                    g = (kt0 // 8) % 2
                    tk = None
                    for kt in range(kt0, kt0 + 8):
                        tk = P.op("tensor", lambda e, k_=kt, g_=g: e.transpose(
                            pTb[g_][:, (k_ % 8) * 128:(k_ % 8) * 128 + 128],
                            vbD[:, k_ * 128:(k_ + 1) * 128], ident[:, :]),
                            waits=([vstate["dv"], tok_vev[g]] if kt == kt0 else []),
                            inc=("sT" if kt == kt0 + 7 else None))
                    dstv = ap3(A_big, kt0 * H + ds * 128,
                               [(NS * T, 128), (H, 8), (1, 128)])
                    tok_vev[g] = P.op("vector", lambda e, d=dstv, g_=g: e.tensor_copy(
                        d, pTb[g_][:, :].rearrange("p (b c) -> p b c", c=128)),
                        writes=("A_bigD",), waits=[tk])
                    vstate["kt"] = kt0 + 8
                    if vstate["kt"] >= NT:
                        vstate["kt"] = 0
                        vstate["ds"] = ds + 1
                        vstate["prev_tr"] = tk

                for t in range(NT):
                    par = t % 2
                    du = P.dma("sync", ub[par][:, :].rearrange("p (b c) -> p b c", c=128),
                               ap3(usp_h, t * 128, [(T, 128), (128 * T, 8), (1, 128)]),
                               "dU0" if par == 0 else "dU1",
                               waits=[tok_umm[par], tok_C_st0, tok_C_st1])
                    dx = P.dma("sync", ptl[par][:, :],
                               x_h.ap()[t * 128:(t + 1) * 128, :],
                               "dA0" if par == 0 else "dA1", waits=[tok_addD[par]])
                    for half in range(2):
                        mb = fb[half]
                        for s in range(NS):
                            tg = P.op("tensor", lambda e, s_=s, h_=half, b=par, m=mb: e.matmul(
                                m[:, :], ub[b][:, s_ * 128:(s_ + 1) * 128],
                                wo_bf[:, s_ * H + h_ * 512:s_ * H + h_ * 512 + 512],
                                start=(s_ == 0), stop=(s_ == NS - 1)),
                                waits=([du, tok_oev[half]] + tok_wo if s == 0 else []),
                                inc=("sT" if s == NS - 1 else None))
                        tok_oev[half] = P.op("scalar", lambda e, h_=half, b=par, m=mb: e.copy(
                            otD[b][:, h_ * 512:h_ * 512 + 512], m[:, :]),
                            writes=(f"otD{par}",),
                            waits=[tg, tok_odmaD[par]])
                    tok_umm[par] = tg
                    tok_addD[par] = P.op("vector", lambda e, t_=t, b=par: e.scalar_tensor_tensor(
                        otD[b][:, :], otD[b][:, :], rms_all[:, t_:t_ + 1],
                        ptl[b][:, :], OP.mult, OP.add),
                        reads=(f"otD{par}", f"ptl{par}", "rms_all"),
                        writes=(f"otD{par}",),
                        waits=[tok_oev[0], tok_oev[1], dx])
                    tok_odmaD[par] = P.dma(
                        "gpsimd", ap3(out_h, t * 128 * H, [(H, 128), (1, H)]),
                        otD[par][:, :], "dG0" if par == 0 else "dG1",
                        waits=[tok_addD[par]])
                    v_step()
                while vstate["ds"] < NS:
                    v_step()
                tok_sev[0] = tok_oev[0]
                tok_sev[1] = tok_oev[1]
            tok_C2_st0 = ("dG0", P.cnt["dG0"])
            tok_C2_st1 = ("dG1", P.cnt["dG1"])
            S_sb = SD("S_sb", [128, T], F32)
            P_bf = [SD(f"P_bf{i}", [128, T], BF16) for i in range(2)]
            PT_sb = SD("PT_sb", [128, T], BF16)

            tok_smm = [None, None]
            tok_exp_par = [None, None]
            tok_ptr_par = [None, None]
            tok_ptev = [tok_vev[0], tok_vev[1]]
            tok_pv = [None, None]
            tok_pvlast = [None]
            tok_riv = [None, None]
            tok_opv = [None, None]
            tok_chmax_prev = None

            def tail_tr(pv):
                b = pv["par"]
                tk = None
                for kt in range(NT):
                    g = (kt // 8) % 2
                    tk = P.op("tensor", lambda e, k_=kt, g_=g, b_=b: e.transpose(
                        pTb[g_][:, (k_ % 8) * 128:(k_ % 8) * 128 + 128],
                        P_bf[b_][:, k_ * 128:(k_ + 1) * 128], ident[:, :]),
                        waits=([pv["exp"], tok_ptev[g]] if kt % 8 == 0 else []),
                        inc=("sT" if kt % 8 == 7 else None))
                    if kt % 8 == 7:
                        kt0 = kt - 7
                        tok_ptev[g] = P.op("vector", lambda e, k0=kt0, g_=g: e.tensor_copy(
                            PT_sb[:, k0 * 128:(k0 + 8) * 128], pTb[g_][:, :]),
                            writes=("PT_sb",), waits=[tk, tok_pvlast[0]])
                tok_ptr_par[b] = tk

            def tail_pv(pv):
                b = pv["par"]
                tt = pv["t"]
                for half in range(2):
                    mb = fb[2 + half]
                    tg = None
                    for kt in range(NT):
                        tg = P.op("tensor", lambda e, k_=kt, h_=half, m=mb: e.matmul(
                            m[:, :], PT_sb[:, k_ * 128:(k_ + 1) * 128],
                            A_big[:, k_ * H + h_ * 512:k_ * H + h_ * 512 + 512],
                            start=(k_ == 0), stop=(k_ == NT - 1)),
                            waits=([tok_ptev[0], tok_ptev[1], tok_pv[half]]
                                   if kt == 0 else []),
                            inc=("sT" if kt == NT - 1 else None))
                    tok_pvlast[0] = tg
                    tok_pv[half] = P.op("scalar", lambda e, h_=half, b_=b, m=mb: e.activation(
                        otD[b_][:, h_ * 512:h_ * 512 + 512], m[:, :], AF.Copy,
                        scale=riv[b_][:, 0:1]),
                        reads=(f"riv{b}",), writes=(f"otD{b}",),
                        waits=[tg, pv["riv"], tok_odmaD[b]])
                    tok_opv[b] = tok_pv[half]
                tok_addD[b] = P.op("vector", lambda e, b_=b: e.tensor_tensor(
                    otD[b_][:, :], otD[b_][:, :], ptl[b_][:, :], OP.add),
                    reads=(f"otD{b}", f"ptl{b}"), writes=(f"otD{b}",),
                    waits=[tok_pv[0], tok_pv[1], pv["dp"]])
                tok_odmaD[b] = P.dma(
                    "gpsimd", ap3(out_h, tt * 128 * H, [(H, 128), (1, H)]),
                    otD[b][:, :], "dG0" if b == 0 else "dG1",
                    waits=[tok_addD[b]])

            prev = None
            for t in range(NT):
                par = t % 2
                if prev is not None:
                    tail_tr(prev)
                dq = P.dma("sync", qb[par][:, :].rearrange("p (b c) -> p b c", c=128),
                           ap3(qkvT_h[0], t * 128, [(T, 128), (128 * T, 8), (1, 128)]),
                           "dA0" if par == 0 else "dA1",
                           waits=[tok_smm[par], tok_B_st0, tok_B_st1])
                dp = P.dma("sync", ptl[par][:, :],
                           ap3(out_h, t * 128 * H, [(H, 128), (1, H)]),
                           "dP0" if par == 0 else "dP1",
                           waits=[tok_addD[par], tok_C2_st0, tok_C2_st1])
                t_chmax = None
                for kc in range(NCH):
                    sb = fb[kc % 2]
                    tg = None
                    for s in range(NS):
                        tg = P.op("tensor", lambda e, s_=s, k_=kc, b=par, m=sb: e.matmul(
                            m[:, :], qb[b][:, s_ * 128:(s_ + 1) * 128],
                            A_xn[:, s_ * T + k_ * 512:s_ * T + k_ * 512 + 512],
                            start=(s_ == 0), stop=(s_ == NS - 1)),
                            waits=([dq, tok_sev[kc % 2], tok_KD] if s == 0 else []),
                            inc=("sT" if s == NS - 1 else None))
                    tok_smm[par] = tg
                    tok_sev[kc % 2] = P.op("scalar", lambda e, k_=kc, m=sb: e.copy(
                        S_sb[:, k_ * 512:k_ * 512 + 512], m[:, :]),
                        writes=("S_sb",),
                        waits=[tg] + ([tok_chmax_prev] if kc == 0 else []))
                    t_chmax = P.op("vector", lambda e, k_=kc, b=par: e.reduce_max(
                        mx8[b][:, k_:k_ + 1], S_sb[:, k_ * 512:k_ * 512 + 512],
                        axis=AX, negate=True),
                        reads=("S_sb",), writes=(f"mx8{par}",),
                        waits=[tok_sev[kc % 2]])
                tok_chmax_prev = t_chmax
                t_rmx = P.op("vector", lambda e, b=par: e.tensor_reduce(
                    ngm[b][:, 0:1], mx8[b][:, :], AX, OP.min),
                    reads=(f"mx8{par}",), writes=(f"ngm{par}",),
                    waits=[t_chmax, tok_exp_par[par]])
                t_bias = P.op("vector", lambda e, b=par: e.tensor_scalar(
                    ngm[b][:, 0:1], ngm[b][:, 0:1], 1.0 / 32.0, None, OP.mult),
                    reads=(f"ngm{par}",), writes=(f"ngm{par}",), waits=[t_rmx])
                tok_exp = P.op("scalar", lambda e, b=par: e.activation(
                    P_bf[b][:, :], S_sb[:, :], AF.Exp, bias=ngm[b][:, 0:1],
                    scale=1.0 / 32.0, accum_out=rsc[b][:, 0:1]),
                    reads=("S_sb",), writes=(f"P_bf{par}", f"rsc{par}"),
                    waits=[t_bias, tok_ptr_par[par], tok_riv[par]])
                tok_exp_par[par] = tok_exp
                if prev is not None:
                    tail_pv(prev)
                t_riv = P.op("vector", lambda e, b=par: e.reciprocal(
                    riv[b][:, 0:1], rsc[b][:, 0:1]),
                    reads=(f"rsc{par}",), writes=(f"riv{par}",),
                    waits=[tok_exp, tok_opv[par]])
                tok_riv[par] = t_riv
                prev = dict(par=par, t=t, exp=tok_exp, riv=t_riv, dp=dp)
            tail_tr(prev)
            tail_pv(prev)
            P.op("gpsimd", lambda e, v0=P.cnt["dG0"], v1=P.cnt["dG1"]: (
                e.wait_ge(P.sems["dG0"], v0), e.wait_ge(P.sems["dG1"], v1))[0],
                 inc=None)
            P.flush()
    return nc


def prep_inputs(inputs):
    d = inputs
    f32 = np.float32
    x = np.ascontiguousarray(np.asarray(d["L0_attn_x"], f32))
    com = {}
    com["wi"] = np.ascontiguousarray(np.asarray(d["L0_attn_Wi"], f32))
    com["wf"] = np.ascontiguousarray(np.asarray(d["L0_attn_Wf"], f32))
    com["wg"] = np.ascontiguousarray(np.asarray(d["L0_attn_Wg"], f32))
    com["wo"] = np.ascontiguousarray(np.asarray(d["L0_attn_Wo"], f32))
    com["ci"] = np.ascontiguousarray(
        np.asarray(d["L0_attn_conv_i"], f32).reshape(8, 128, 4)
        .transpose(1, 0, 2).reshape(128, 32))
    com["cf"] = np.ascontiguousarray(
        np.asarray(d["L0_attn_conv_f"], f32).reshape(8, 128, 4)
        .transpose(1, 0, 2).reshape(128, 32))
    com["gnw"] = np.ascontiguousarray(
        np.asarray(d["L0_attn_g_norm_weight"], f32).reshape(8, 128).T)
    for f in ("q", "k", "v"):
        com[f + "w1"] = np.ascontiguousarray(
            np.asarray(d[f"L0_ffn_{f}_w1"], f32).transpose(1, 0, 2).reshape(H, H))
        com[f + "b1"] = np.ascontiguousarray(
            np.asarray(d[f"L0_ffn_{f}_b1"], f32).reshape(8, 128).T)
        com[f + "w2"] = np.ascontiguousarray(
            np.asarray(d[f"L0_ffn_{f}_w2"], f32).reshape(H, H))
        b2p = np.zeros((128, H), f32)
        b2p[0:16] = np.asarray(d[f"L0_ffn_{f}_b2"], f32)
        com[f + "b2"] = b2p
        nwp = np.zeros((H, 16), f32)
        nwp[:, 0:15] = np.asarray(d[f"L0_ffn_{f}_node_w"], f32).T
        com[f + "nwT"] = nwp
        nbp = np.zeros((1, 16), f32)
        nbp[0, 0:15] = np.asarray(d[f"L0_ffn_{f}_node_b"], f32)
        com[f + "nb"] = nbp
    com["ident"] = np.eye(128, dtype=BF)
    ep = np.zeros((128, H), dtype=BF)
    for p in range(8):
        ep[2 * p, p * 128:p * 128 + 64] = 1
        ep[2 * p + 1, p * 128 + 64:p * 128 + 128] = 1
    com["epairs"] = ep
    com["iota16"] = np.tile(np.arange(16, dtype=f32), (128, 1))
    com["onesrow"] = np.ones((1, 128), dtype=BF)
    maps = []
    for c in range(NCORE):
        m = dict(com)
        m["x"] = np.ascontiguousarray(x[c])
        maps.append(m)
    return maps


def kernel(**inputs):
    maps = prep_inputs(inputs)
    nc = build_program()
    res = run_bass_kernel_spmd(nc, maps, core_ids=list(range(NCORE)))
    return np.stack([r["out"] for r in res.results], axis=0)


# version 15 (final)
# speedup vs baseline: 1.1890x; 1.0045x over previous
import numpy as np
import ml_dtypes
import concourse.bass as bass
import concourse.mybir as mybir
from concourse.bass_utils import run_bass_kernel_spmd
from contextlib import ExitStack

BF = ml_dtypes.bfloat16
F32 = mybir.dt.float32
BF16 = mybir.dt.bfloat16
OP = mybir.AluOpType
AF = mybir.ActivationFunctionType
AX = mybir.AxisListType.X

T = 4096
H = 1024
NS = 8
NT = 32
NCH = 8
NCORE = 8
ENG2SEM = {"tensor": "sT", "vector": "sV", "scalar": "sS"}


DMA_SEMS = ("dA0", "dA1", "dA2", "dU0", "dU1", "dP0", "dP1", "dW0", "dW1",
            "dN", "dNB", "dB1", "dB2", "dK", "dG0", "dG1", "dDBG",
            "dC0", "dC1", "dC2", "dC3", "dC4", "dC5", "dC6", "dC7", "dW2")


class Prog:
    ENGINES = ("sync", "gpsimd", "tensor", "vector", "scalar")

    def __init__(self, nc):
        self.nc = nc
        self.es = ExitStack()
        self.sems = {}
        self.cnt = {}
        for name in ("sT", "sV", "sS") + DMA_SEMS:
            self.sems[name] = self.es.enter_context(nc.semaphore(name))
            self.cnt[name] = 0
        self.streams = {e: [] for e in self.ENGINES}
        self.waited = {e: {} for e in self.ENGINES}
        self.dirtyW = {e: set() for e in self.ENGINES}
        self.dirtyR = {e: set() for e in self.ENGINES}

    def op(self, eng, fn, reads=(), writes=(), waits=(), inc="auto"):
        drain = False
        if eng in ("vector", "scalar", "gpsimd"):
            if (set(reads) & self.dirtyW[eng]) or (set(writes) & self.dirtyW[eng]) \
               or (set(writes) & self.dirtyR[eng]):
                drain = True
                self.dirtyW[eng] = set()
                self.dirtyR[eng] = set()
            self.dirtyW[eng] |= set(writes)
            self.dirtyR[eng] |= set(reads)
        if inc == "auto":
            inc = ENG2SEM[eng] if eng in ("vector", "scalar") else None
        tok = None
        amt = 1
        if inc is not None:
            self.cnt[inc] += amt
            tok = (inc, self.cnt[inc])
        own = ENG2SEM.get(eng)
        w = tuple((s, v) for (s, v) in [x for x in waits if x is not None]
                  if s != own)
        self.streams[eng].append((w, drain, fn, (inc, amt) if inc else None))
        return tok

    def dma(self, queue, dst, src, sem, waits=()):
        self.cnt[sem] += 16
        tok = (sem, self.cnt[sem])
        w = tuple((s, v) for (s, v) in [x for x in waits if x is not None])
        self.streams[queue].append(
            (w, False, lambda e, d=dst, s_=src: e.dma_start(d, s_), (sem, 16)))
        return tok

    def fence_gpsimd(self, sems):
        vals = [(s, self.cnt[s]) for s in sems if self.cnt[s] > 0]
        if not vals:
            return
        def fn(e, vv=tuple(vals)):
            inst = None
            for s, v in vv:
                inst = e.wait_ge(self.sems[s], v)
            return inst
        self.op("gpsimd", fn, inc=None)

    def flush(self):
        with self.nc.Block(no_gpsimd_drain=True) as blk:
            for eng in self.ENGINES:
                ops = self.streams[eng]
                if not ops:
                    continue

                def make(ops_, eng_):
                    def emit(engine):
                        w8 = self.waited[eng_]
                        for waits, drain, fn, inc in ops_:
                            for (sn, val) in waits:
                                if w8.get(sn, 0) >= val:
                                    continue
                                engine.wait_ge(self.sems[sn], val)
                                w8[sn] = val
                            if drain:
                                engine.drain()
                            inst = fn(engine)
                            if inc is not None:
                                inst.then_inc(self.sems[inc[0]], inc[1])
                    return emit

                getattr(blk, eng)(make(list(ops), eng))
                self.streams[eng] = []
        for e in self.ENGINES:
            for s in ("sT", "sV", "sS"):
                self.waited[e][s] = max(self.waited[e].get(s, 0), self.cnt[s])
            self.dirtyW[e] = set()
            self.dirtyR[e] = set()


def ap3(handle, offset, dims):
    return bass.AP(handle, offset, [list(d) for d in dims])


def build_program(upto="D", dbg=()):
    nc = bass.Bass("TRN2", target_bir_lowering=False)
    DR = lambda name, shape, dt, kind="ExternalInput": nc.dram_tensor(
        name, shape, dt, kind=kind)

    x_h = DR("x", [T, H], F32)
    g_h = {k: DR(k, [H, H], F32) for k in ("wi", "wf", "wg", "wo")}
    ci_h = DR("ci", [128, 32], F32)
    cf_h = DR("cf", [128, 32], F32)
    gnw_h = DR("gnw", [128, 8], F32)
    fff_h = {}
    for f in ("q", "k", "v"):
        fff_h[f] = dict(
            w1=DR(f + "w1", [H, H], F32), b1=DR(f + "b1", [128, 8], F32),
            w2=DR(f + "w2", [H, H], F32), b2=DR(f + "b2", [128, H], F32),
            nwT=DR(f + "nwT", [H, 16], F32), nb=DR(f + "nb", [1, 16], F32))
    ident_h = DR("ident", [128, 128], BF16)
    epairs_h = DR("epairs", [128, H], BF16)
    iota16_h = DR("iota16", [128, 16], F32)
    onesrow_h = DR("onesrow", [1, 128], BF16)

    qkvT_h = [DR(n, [H, T], BF16, "Internal") for n in ("qTs", "kTs", "vTs")]
    wbf_h = {n: DR(n + "_bf", [H, H], BF16, "Internal")
             for n in ("wi", "wf", "wg", "wo", "qw1", "qw2", "kw1", "kw2",
                       "vw1", "vw2")}
    usp_h = DR("usp", [H, T], BF16, "Internal")
    out_h = DR("out", [T, H], F32, "ExternalOutput")
    dbg_h = {}
    if "xnT" in dbg:
        dbg_h["xnT"] = DR("dbg_xnT", [128, NS * T], BF16, "ExternalOutput")
    if "qkv" in dbg:
        for i, f in enumerate(("q", "k", "v")):
            dbg_h["qT_" + f] = DR("dbg_qT_" + f, [H, T], BF16, "ExternalOutput")
    if "u" in dbg:
        dbg_h["usp"] = DR("dbg_usp", [H, T], BF16, "ExternalOutput")
        dbg_h["rms"] = DR("dbg_rms", [128, 32], F32, "ExternalOutput")

    es = ExitStack()
    S_ = lambda name, shape, dt: es.enter_context(nc.sbuf_tensor(name, shape, dt))
    P_ = lambda name, shape, dt: es.enter_context(nc.psum_tensor(name, shape, dt))
    with es:
        A_xn = S_("A_xn", [128, NS * T], BF16)
        ident = S_("identS", [128, 128], BF16)
        epairs = S_("epairsS", [128, H], BF16)
        iota16 = S_("iota16S", [128, 16], F32)
        onesrow = S_("onesrowS", [1, 128], BF16)
        onescol = S_("onescolS", [128, 1], F32)
        epscol = S_("epscolS", [128, 1], F32)
        rms_all = S_("rms_all", [128, 32], F32)
        pA = P_("pA", [128, 512], F32)
        pB = P_("pB", [128, 512], F32)
        pC = P_("pC", [128, 512], F32)
        pD = P_("pD", [128, 512], F32)
        pS = P_("pS", [128, 512], F32)
        pT0 = P_("pT0", [128, 1024], BF16)
        pT1 = P_("pT1", [128, 1024], BF16)
        fb = [pA, pB, pC, pD]
        pTb = [pT0, pT1]

        P = Prog(nc)

        P.dma("sync", ident[:, :], ident_h.ap(), "dK")
        P.dma("sync", epairs[:, :], epairs_h.ap(), "dK")
        P.dma("sync", iota16[:, :], iota16_h.ap(), "dK")
        P.dma("sync", onesrow[:, :], onesrow_h.ap(), "dK")
        tok_K = ("dK", P.cnt["dK"])
        P.op("vector", lambda e: e.memset(onescol[:, :], 1.0), writes=("onescol",))
        P.op("vector", lambda e: e.memset(epscol[:, :], 1e-5), writes=("epscol",))

        esA = ExitStack()
        with esA:
            SA = lambda n, sh, dt: esA.enter_context(nc.sbuf_tensor(n, sh, dt))
            xt = [SA(f"xt{i}", [128, H], F32) for i in range(3)]
            dump = SA("dumpA", [128, H], F32)
            xnb = [SA(f"xnb{i}", [128, H], BF16) for i in range(2)]
            cS = [SA(f"cS{i}", [128, 1], F32) for i in range(2)]
            cQ = [SA(f"cQ{i}", [128, 1], F32) for i in range(2)]
            cNM = [SA(f"cNM{i}", [128, 1], F32) for i in range(2)]
            cM2 = [SA(f"cM2{i}", [128, 1], F32) for i in range(2)]
            cV = [SA(f"cV{i}", [128, 1], F32) for i in range(2)]
            cR = [SA(f"cR{i}", [128, 1], F32) for i in range(2)]
            tok_tr = [None, None]
            tok_ev = [None, None]
            tok_nrm = [None, None]
            tok_nrm3 = [None, None, None]
            prevA = None

            def tailA(pv):
                b = pv["par"]
                xi_ = pv["xi"]
                t = pv["t"]
                P.op("vector", lambda e, b_=b: e.reciprocal(cR[b_][:, 0:1], cR[b_][:, 0:1]),
                     reads=(f"cR{b}",), writes=(f"cR{b}",), waits=[pv["sqrt"]])
                tnorm = P.op("vector", lambda e, b_=b, x_=xi_: e.tensor_scalar(
                    xnb[b_][:, :], xt[x_][:, :], cNM[b_][:, 0:1], cR[b_][:, 0:1],
                    OP.add, OP.mult),
                    reads=(f"xt{xi_}", f"cNM{b}", f"cR{b}"), writes=(f"xnb{b}",),
                    waits=[tok_tr[b]])
                tok_nrm[b] = tnorm
                tok_nrm3[xi_] = tnorm
                tk = None
                for s in range(NS):
                    tk = P.op("tensor", lambda e, b_=b, s_=s: e.transpose(
                        pTb[b_][:, s_ * 128:(s_ + 1) * 128],
                        xnb[b_][:, s_ * 128:(s_ + 1) * 128], ident[:, :]),
                        waits=([tnorm, tok_ev[b], tok_K] if s == 0 else []),
                        inc=("sT" if s == NS - 1 else None))
                tok_tr[b] = tk
                dst = ap3(A_xn, t * 128, [(NS * T, 128), (T, 8), (1, 128)])
                tok_ev[b] = P.op("scalar", lambda e, b_=b, d=dst: e.copy(
                    d, pTb[b_][:, :].rearrange("p (b c) -> p b c", c=128)),
                    writes=("A_xn",), waits=[tk])

            dmas = {}
            XSEM = ("dA0", "dA1", "dA2")
            for t0 in range(2):
                dmas[t0] = P.dma("sync", xt[t0 % 3][:, :],
                                 x_h.ap()[t0 * 128:(t0 + 1) * 128, :],
                                 XSEM[t0 % 3], waits=[])
            for t in range(NT):
                bi = t % 2
                xi = t % 3
                if prevA is not None:
                    tailA(prevA)
                if t + 2 < NT:
                    tp2 = t + 2
                    dmas[tp2] = P.dma("sync", xt[tp2 % 3][:, :],
                                      x_h.ap()[tp2 * 128:(tp2 + 1) * 128, :],
                                      XSEM[tp2 % 3],
                                      waits=[tok_nrm3[tp2 % 3]])
                dq = dmas[t]
                xb = f"xt{xi}"
                P.op("vector", lambda e, b=xi, c=bi: e.reduce_sum(
                    cS[c][:, 0:1], xt[b][:, :], axis=AX),
                    reads=(xb,), writes=(f"cS{bi}",), waits=[dq])
                tsqx = P.op("scalar", lambda e, b=xi, c=bi: e.activation(
                    dump[:, :], xt[b][:, :], AF.Square,
                    accum_out=cQ[c][:, 0:1]),
                    reads=(xb,), writes=("dumpA", f"cQ{bi}"), waits=[dq])
                P.op("vector", lambda e, b=bi: e.tensor_scalar(
                    cNM[b][:, 0:1], cS[b][:, 0:1], -1.0 / H, None, OP.mult),
                    reads=(f"cS{bi}",), writes=(f"cNM{bi}",))
                P.op("vector", lambda e, b=bi: e.tensor_tensor(
                    cM2[b][:, 0:1], cNM[b][:, 0:1], cNM[b][:, 0:1], OP.mult),
                    reads=(f"cNM{bi}",), writes=(f"cM2{bi}",))
                tva = P.op("vector", lambda e, b=bi: e.scalar_tensor_tensor(
                    cV[b][:, 0:1], cQ[b][:, 0:1], 1.0 / H, cM2[b][:, 0:1],
                    OP.mult, OP.subtract),
                    reads=(f"cQ{bi}", f"cM2{bi}"), writes=(f"cV{bi}",),
                    waits=[tsqx])
                tsq = P.op("scalar", lambda e, b=bi: e.activation(
                    cR[b][:, 0:1], cV[b][:, 0:1], AF.Sqrt, bias=epscol[:, 0:1]),
                    reads=(f"cV{bi}", "epscolS"), writes=(f"cR{bi}",),
                    waits=[tva, tok_nrm[bi]])
                prevA = dict(par=bi, xi=xi, t=t, sqrt=tsq)
            tailA(prevA)
            if "xnT" in dbg:
                P.dma("gpsimd", dbg_h["xnT"].ap(), A_xn[:, :], "dDBG",
                      waits=[tok_ev[0], tok_ev[1]])
            P.flush()

        if upto == "A":
            return nc

        tok_qkv_store = None
        esB = ExitStack()
        with esB:
            SB = lambda n, sh, dt: esB.enter_context(nc.sbuf_tensor(n, sh, dt))
            A_big = SB("A_big", [128, NS * T], BF16)
            selT = SB("selT", [128, T], BF16)
            S_all = SB("S_allB", [128, 512], F32)
            wld = [SB(f"wldB{i}", [128, H], BF16) for i in range(2)]
            w2ld = [SB(f"w2ldB{i}", [128, H], BF16) for i in range(2)]
            cwl = [SB(f"cwl{i}", [128, H], F32) for i in range(4)]
            cwb = [SB(f"cwb{i}", [128, H], BF16) for i in range(4)]
            cast_q = []
            b1c = SB("b1c", [128, 8], F32)
            b2bf = SB("b2bf", [128, H], BF16)
            b2ld = SB("b2ld", [128, H], F32)
            nwbf = SB("nwbf", [128, 128], BF16)
            nwld = SB("nwld", [128, 128], F32)
            nbbf = SB("nbbf", [1, 16], BF16)
            nbld = SB("nbld", [1, 16], F32)
            Hf = [SB(f"Hf{i}", [128, 512], F32) for i in range(5)]
            qst = [SB(f"qst{i}", [128, T], BF16) for i in range(2)]
            tr1 = SB("tr1", [128, 32], F32)
            tr2 = SB("tr2", [128, 32], F32)
            tr3 = SB("tr3", [128, 32], F32)
            bb0 = SB("bb0", [128, 32], F32)
            bb1 = SB("bb1", [128, 32], F32)
            bb2 = SB("bb2", [128, 32], F32)
            leaf = SB("leafB", [128, 32], F32)
            ohb = SB("ohb", [128, 16], BF16)

            tz_sel = P.op("vector", lambda e: e.memset(selT[:, :], 0.0),
                          writes=("selT",))
            cast_st = {}
            CLSEM = ("dC0", "dC1", "dC2", "dC3")
            CSSEM = ("dC4", "dC5", "dC6", "dC7")
            cast_state = dict(ld=[None] * 4, cst=[None] * 4, stp=0)

            def cast_step():
                if not cast_q:
                    return
                name, s8 = cast_q.pop(0)
                src_h = g_h[name] if name in g_h else \
                    fff_h[name[0]]["w1" if name.endswith("1") else "w2"]
                i = cast_state["stp"]
                cast_state["stp"] += 1
                par = i % 4
                dl = P.dma("sync", cwl[par][:, :],
                           ap3(src_h, s8 * 128 * H, [(H, 128), (1, H)]),
                           CLSEM[par], waits=[cast_state["cst"][par]])
                eng = "vector" if i % 2 == 0 else "scalar"
                if eng == "vector":
                    ct = P.op("vector", lambda e, b=par: e.tensor_copy(
                        cwb[b][:, :], cwl[b][:, :]),
                        reads=(f"cwl{par}",), writes=(f"cwb{par}",),
                        waits=[dl, cast_state["ld"][par]])
                else:
                    ct = P.op("scalar", lambda e, b=par: e.copy(
                        cwb[b][:, :], cwl[b][:, :]),
                        reads=(f"cwl{par}",), writes=(f"cwb{par}",),
                        waits=[dl, cast_state["ld"][par]])
                cast_state["cst"][par] = ct
                last = P.dma("gpsimd",
                             ap3(wbf_h[name], s8 * 128 * H, [(H, 128), (1, H)]),
                             cwb[par][:, :], CSSEM[par], waits=[ct])
                cast_state["ld"][par] = last
                if s8 == NS - 1:
                    cast_st[name] = tuple((sm, P.cnt[sm]) for sm in CSSEM)

            def queue_wcast(name):
                for s8 in range(NS):
                    cast_q.append((name, s8))
                cast_st.pop(name, None)

            def drain_wcast(name):
                while name not in cast_st:
                    cast_step()

            queue_wcast("qw1")
            queue_wcast("qw2")
            drain_wcast("qw1")
            drain_wcast("qw2")
            tok_qst = [None, None]
            tok_qev = [None, None]
            tok_w1c = [None, None]
            tok_w1te = [None, None]
            tok_w2te = [None, None]
            tok_hev = [None, None]
            tok_mmul = [None] * 5
            tok_hbank = [None, None]
            tok_kbank = [None, None]
            tok_tev_prev = None
            tok_tlf_prev = None
            tok_scmm_prev = None
            tok_w2mm_prev = None
            c1 = c2 = c3 = None
            for fi, f in enumerate(("q", "k", "v")):
                hh = fff_h[f]
                dst_h = qkvT_h[fi]
                nw_src = ap3(hh["nwT"], 0, [(16, 128), (16 * 128, 8), (1, 16)])
                t_nw = P.dma("sync", nwld[:, :].rearrange("p (b c) -> p b c", c=16),
                             nw_src, "dN", waits=[c1])
                t_nb = P.dma("sync", nbld[:, :], hh["nb"].ap(), "dNB", waits=[c2])
                t_b1 = P.dma("sync", b1c[:, :], hh["b1"].ap(), "dB1",
                             waits=[tok_hev[0], tok_hev[1]])
                t_b2 = P.dma("sync", b2ld[:, :], hh["b2"].ap(), "dB2", waits=[c3])
                c1 = P.op("vector", lambda e: e.tensor_copy(nwbf[:, :], nwld[:, :]),
                          reads=("nwld",), writes=("nwbf",),
                          waits=[t_nw, tok_scmm_prev])
                c2 = P.op("vector", lambda e: e.tensor_copy(nbbf[:, :], nbld[:, :]),
                          reads=("nbld",), writes=("nbbf",),
                          waits=[t_nb, tok_scmm_prev])
                c3 = P.op("vector", lambda e: e.tensor_copy(b2bf[:, :], b2ld[:, :]),
                          reads=("b2ld",), writes=("b2bf",),
                          waits=[t_b2, tok_w2mm_prev])
                tok_sc = None
                for t in range(NT):
                    for s in range(NS):
                        P.op("tensor", lambda e, t_=t, s_=s: e.matmul(
                            pS[:, 16 * t_:16 * t_ + 16],
                            A_xn[:, s_ * T + t_ * 128:s_ * T + t_ * 128 + 128],
                            nwbf[:, s_ * 16:s_ * 16 + 16],
                            start=(s_ == 0), stop=False),
                            waits=([c1, c2, tok_K, tok_tev_prev]
                                   if (t == 0 and s == 0) else []),
                            inc=None)
                    tok_sc = P.op("tensor", lambda e, t_=t: e.matmul(
                        pS[:, 16 * t_:16 * t_ + 16], onesrow[:, :], nbbf[:, :],
                        start=False, stop=True),
                        inc=("sT" if t == NT - 1 else None))
                tok_scmm_prev = tok_sc
                tev = P.op("scalar", lambda e: e.copy(S_all[:, :], pS[:, :]),
                           writes=("S_allB",), waits=[tok_sc, tok_tlf_prev])
                tok_tev_prev = tev
                col = lambda j: ap3(S_all, j, [(512, 128), (16, 32)])
                V = "vector"

                def tt(d, a, b_, o, extra_w=()):
                    return P.op(V, lambda e: e.tensor_tensor(d[1], a[1], b_[1], o),
                                reads=(a[0], b_[0]), writes=(d[0],), waits=extra_w)

                def ge(d, a, extra_w=()):
                    return P.op(V, lambda e: e.tensor_scalar(
                        d[1], a[1], 0.0, None, OP.is_ge),
                        reads=(a[0],), writes=(d[0],), waits=extra_w)

                SA_ = lambda j: ("S_allB", col(j))
                TR1 = ("tr1", tr1[:, :]); TR2 = ("tr2", tr2[:, :])
                TR3 = ("tr3", tr3[:, :])
                B0 = ("bb0", bb0[:, :]); B1 = ("bb1", bb1[:, :])
                B2 = ("bb2", bb2[:, :])
                ge(B0, SA_(0), extra_w=[tev])

                def lerp(dst, lo, hi, bit):
                    tt(dst, hi, lo, OP.subtract)
                    tt(dst, dst, bit, OP.mult)
                    tt(dst, dst, lo, OP.add)
                lerp(TR1, SA_(1), SA_(2), B0)
                ge(B1, TR1)
                lerp(TR1, SA_(3), SA_(4), B1)
                lerp(TR2, SA_(5), SA_(6), B1)
                lerp(TR2, TR1, TR2, B0)
                ge(B2, TR2)
                lerp(TR1, SA_(7), SA_(8), B2)
                lerp(TR2, SA_(9), SA_(10), B2)
                lerp(TR2, TR1, TR2, B1)
                lerp(TR1, SA_(11), SA_(12), B2)
                lerp(TR3, SA_(13), SA_(14), B2)
                lerp(TR3, TR1, TR3, B1)
                lerp(TR3, TR2, TR3, B0)
                ge(TR3, TR3)
                P.op(V, lambda e: e.scalar_tensor_tensor(
                    leaf[:, :], bb0[:, :], 2.0, bb1[:, :], OP.mult, OP.add),
                    reads=("bb0", "bb1"), writes=("leafB",))
                P.op(V, lambda e: e.scalar_tensor_tensor(
                    leaf[:, :], leaf[:, :], 2.0, bb2[:, :], OP.mult, OP.add),
                    reads=("leafB", "bb2"), writes=("leafB",))
                tlf = P.op(V, lambda e: e.scalar_tensor_tensor(
                    leaf[:, :], leaf[:, :], 2.0, tr3[:, :], OP.mult, OP.add),
                    reads=("leafB", "tr3"), writes=("leafB",))
                tok_tlf_prev = tlf
                last_h = None
                w1name = f + "w1"
                drain_wcast(w1name)
                tok_hev_slot = {}
                t_h_p0 = [None]

                def w1_load(p_):
                    bi_ = p_ % 2
                    wsrc = ap3(wbf_h[w1name], p_ * 128,
                               [(H, 128), (128 * H, 8), (1, 128)])
                    tok_w1c[bi_] = P.dma(
                        "sync", wld[bi_][:, :].rearrange("p (b c) -> p b c", c=128),
                        wsrc, "dW0" if bi_ == 0 else "dW1",
                        waits=[tok_w1te[bi_], tok_w2te[bi_]] + list(cast_st[w1name]))

                def h_group(p_, ch_, hslot):
                    bi_ = p_ % 2
                    par = (p_ * NCH + ch_) % 2
                    mb = fb[par]
                    tkh = None
                    for s in range(NS):
                        tkh = P.op("tensor", lambda e, s_=s, c_=ch_, b=bi_, m=mb: e.matmul(
                            m[:, :], wld[b][:, s_ * 128:(s_ + 1) * 128],
                            A_xn[:, s_ * T + c_ * 512:s_ * T + c_ * 512 + 512],
                            start=(s_ == 0), stop=(s_ == NS - 1)),
                            waits=([tok_w1c[bi_], tok_hbank[par], tok_qev[par]]
                                   if s == 0 else []),
                            inc=("sT" if s == NS - 1 else None))
                    t_he = P.op("scalar", lambda e, q_=p_, hs=hslot, m=mb: e.activation(
                        Hf[hs][:, :], m[:, :], AF.Relu, bias=b1c[:, q_:q_ + 1]),
                        reads=("b1c",), writes=(f"Hf{hslot}",),
                        waits=[tkh, t_b1, tok_mmul[hslot]])
                    tok_hev[par] = t_he
                    tok_hbank[par] = t_he
                    tok_hev_slot[hslot] = t_he
                    return tkh

                def m_group(p_, ch_, hslot):
                    par = (p_ * NCH + ch_) % 2
                    kb = fb[2 + par]
                    t_m = P.op("tensor", lambda e, q_=p_, c_=ch_, k=kb: e.matmul(
                        k[:, :], epairs[:, q_ * 128:(q_ + 1) * 128],
                        selT[:, c_ * 512:c_ * 512 + 512],
                        start=True, stop=True),
                        waits=[tok_sel[0], tok_kbank[par]], inc="sT")
                    t_he = tok_hev_slot[hslot]
                    t_mm = P.op(V, lambda e, q_=p_, c_=ch_, hs=hslot, k=kb: e.tensor_tensor(
                        A_big[:, q_ * T + c_ * 512:q_ * T + c_ * 512 + 512],
                        Hf[hs][:, :], k[:, :], OP.mult),
                        reads=(f"Hf{hslot}",), writes=("A_big",),
                        waits=[t_he, t_m])
                    tok_mmul[hslot] = t_mm
                    tok_kbank[par] = t_mm
                    return t_mm

                tok_sel = [None]
                L = 5
                w1_load(0)
                for ch in range(L):
                    t_h_p0[0] = h_group(0, ch, ch % L)
                tok_ohT = None
                tok_selev = [None, None]
                first_sel = True
                for t in range(NT):
                    gi = (t // 8) % 2
                    toh = P.op(V, lambda e, t_=t: e.tensor_scalar(
                        ohb[:, :], iota16[:, :], leaf[:, t_:t_ + 1], None,
                        OP.is_equal),
                        reads=("iota16S", "leafB"), writes=("ohb",),
                        waits=[tlf, tok_ohT])
                    tok_ohT = P.op("tensor", lambda e, t_=t, g=gi: e.transpose(
                        pTb[g][0:16, (t_ % 8) * 128:(t_ % 8) * 128 + 128],
                        ohb[:, :], ident[:, :]),
                        waits=[toh] + ([tok_selev[gi]] if t % 8 == 0 else []),
                        inc="sT")
                    if t % 8 == 7:
                        t0 = t - 7
                        w_extra = [tok_w2mm_prev, tz_sel] if first_sel else []
                        first_sel = False
                        tok_selev[gi] = P.op(V, lambda e, t0_=t0, g=gi: e.tensor_copy(
                            selT[0:16, t0_ * 128:(t0_ + 8) * 128], pTb[g][0:16, :]),
                            writes=("selT",), waits=[tok_ohT] + w_extra)
                tok_sel[0] = tok_selev[1]
                for wname in (("kw1", "kw2", "wi") if fi == 0 else
                              ("vw1", "vw2", "wf") if fi == 1 else
                              ("wg", "wo")):
                    queue_wcast(wname)
                for ch in range(L, NCH):
                    m_group(0, ch - L, (ch - L) % L)
                    t_h_p0[0] = h_group(0, ch, ch % L)
                for ch in range(NCH - L, NCH):
                    last_h = m_group(0, ch, ch % L)
                tok_w1te[0] = t_h_p0[0]
                for p in range(1, NS):
                    bi = p % 2
                    cast_step()
                    cast_step()
                    w1_load(p)
                    t_h = None
                    for ch in range(NCH):
                        t_h = h_group(p, ch, ch % 2)
                        last_h = m_group(p, ch, ch % 2)
                    tok_w1te[bi] = t_h
                tq = None
                w2name = f + "w2"
                drain_wcast(w2name)
                for dch in range(NS):
                    bi = dch % 2
                    cast_step()
                    wsrc = ap3(wbf_h[w2name], dch * 128,
                               [(H, 128), (128 * H, 8), (1, 128)])
                    tok_w2c2 = P.dma(
                        "sync", w2ld[bi][:, :].rearrange("p (b c) -> p b c", c=128),
                        wsrc, "dW0" if bi == 0 else "dW1",
                        waits=[tok_w2te[bi], tok_w1c[bi]] + list(cast_st[w2name]))
                    sp = dch % 2
                    for ch in range(NCH):
                        par = (dch * NCH + ch) % 2
                        mb = fb[par]
                        for p in range(NS):
                            P.op("tensor", lambda e, p_=p, ch_=ch, b=bi, m=mb: e.matmul(
                                m[:, :], w2ld[b][:, p_ * 128:(p_ + 1) * 128],
                                A_big[:, p_ * T + ch_ * 512:p_ * T + ch_ * 512 + 512],
                                start=(p_ == 0), stop=False),
                                waits=([tok_w2c2, last_h, c3, tok_qev[par]]
                                       if p == 0 else []), inc=None)
                        tq = P.op("tensor", lambda e, ch_=ch, d_=dch, m=mb: e.matmul(
                            m[:, :], b2bf[:, d_ * 128:d_ * 128 + 128],
                            selT[:, ch_ * 512:ch_ * 512 + 512],
                            start=False, stop=True), inc="sT")
                        tok_qev[par] = P.op("scalar", lambda e, ch_=ch, s=sp, m=mb: e.copy(
                            qst[s][:, ch_ * 512:ch_ * 512 + 512], m[:, :]),
                            writes=(f"qst{sp}",),
                            waits=[tq] + ([tok_qst[sp]] if ch == 0 else []))
                    tok_w2te[bi] = tq
                    tok_qst[sp] = P.dma(
                        "gpsimd", ap3(dst_h, dch * 128 * T, [(T, 128), (1, T)]),
                        qst[sp][:, :], "dG0" if sp == 0 else "dG1",
                        waits=[tok_qev[0], tok_qev[1]])
                tok_w2mm_prev = tq
                if "qkv" in dbg:
                    P.dma("gpsimd", dbg_h["qT_" + f].ap(), dst_h.ap(), "dDBG",
                          waits=[tok_qst[0], tok_qst[1]])
            while cast_q:
                cast_step()
            P.fence_gpsimd(("dC4", "dC5", "dC6", "dC7", "dG0", "dG1"))
            P.flush()
        tok_B_st0 = ("dG0", P.cnt["dG0"])
        tok_B_st1 = ("dG1", P.cnt["dG1"])

        if upto == "B":
            return nc

        esC = ExitStack()
        with esC:
            SC = lambda n, sh, dt: esC.enter_context(nc.sbuf_tensor(n, sh, dt))
            C1 = SC("C1", [128, T + 4], F32)
            C2 = SC("C2", [128, T + 4], F32)
            C3 = SC("C3", [128, T], F32)
            C4 = SC("C4", [128, T], F32)
            C5 = SC("C5", [128, T], F32)
            sqb = SC("sqb", [128, T], BF16)
            wldC = [SC(f"wldC{i}", [128, H], BF16) for i in range(3)]
            cicol = SC("cicol", [128, 32], F32)
            cfcol = SC("cfcol", [128, 32], F32)
            gnwc = SC("gnwc", [128, 8], F32)
            ssq = SC("ssq", [128, 32], F32)
            onescol_bf = SC("onescol_bf", [128, 1], BF16)
            ust = [SC(f"ust{i}", [128, T], BF16) for i in range(2)]

            P.dma("sync", cicol[:, :], ci_h.ap(), "dK")
            P.dma("sync", cfcol[:, :], cf_h.ap(), "dK")
            P.dma("sync", gnwc[:, :], gnw_h.ap(), "dK")
            tok_KC = ("dK", P.cnt["dK"])
            d_ci = d_cf = d_gn = tok_KC
            tz1 = P.op("vector", lambda e: e.memset(C1[:, 0:3], 0.0), writes=("C1",))
            tz2 = P.op("vector", lambda e: e.memset(C2[:, 0:3], 0.0), writes=("C2",))
            tz3 = P.op("vector", lambda e: e.memset(ssq[:, :], 0.0), writes=("ssq",))
            tzo = P.op("vector", lambda e: e.memset(onescol_bf[:, :], 1.0),
                       writes=("onescol_bf",))

            gates = [("wi", C1, 3, "C1"), ("wf", C2, 3, "C2"), ("wg", C3, 0, "C3")]
            st_c = dict(wload=[None, None, None], wte=[None, None, None],
                        bank=[None, None, None, None],
                        convi=None, convf=None, u=None, scan=None, sq=None,
                        ssqmm=None, udma=[None, None], ssqacc=None)
            evs = {}

            WSEM = ("dW0", "dW1", "dW2")

            def emit_wload(s, gidx):
                gname = gates[gidx][0]
                par = (s * 3 + gidx) % 3
                wsrc = ap3(wbf_h[gname], s * 128, [(H, 128), (128 * H, 8), (1, 128)])
                st_c["wload"][par] = P.dma(
                    "sync", wldC[par][:, :].rearrange("p (b c) -> p b c", c=128),
                    wsrc, WSEM[par],
                    waits=[st_c["wte"][par]] + list(cast_st[gname]))

            def emit_group(s, gidx, ch):
                par = (s * 3 + gidx) % 3
                bpar = (gidx * NCH + ch) % 4
                mb = fb[bpar]
                dstbuf, ofs, bufname = gates[gidx][1], gates[gidx][2], gates[gidx][3]
                tg = None
                for hs in range(NS):
                    tg = P.op("tensor", lambda e, h_=hs, c_=ch, b=par, m=mb: e.matmul(
                        m[:, :], wldC[b][:, h_ * 128:(h_ + 1) * 128],
                        A_xn[:, h_ * T + c_ * 512:h_ * T + c_ * 512 + 512],
                        start=(h_ == 0), stop=(h_ == NS - 1)),
                        waits=([st_c["wload"][par], st_c["bank"][bpar]]
                               if hs == 0 else []),
                        inc=("sT" if hs == NS - 1 else None))
                st_c["wte"][par] = tg
                if gidx == 0:
                    dfree = [st_c["convi"], tz1]
                elif gidx == 1:
                    dfree = [st_c["convf"], tz2]
                else:
                    dfree = [st_c["u"]]
                tev = P.op("scalar", lambda e, d=dstbuf, o=ofs, c_=ch, m=mb: e.copy(
                    d[:, o + c_ * 512:o + c_ * 512 + 512], m[:, :]),
                    writes=(bufname,), waits=[tg] + dfree)
                st_c["bank"][bpar] = tev
                evs.setdefault(s, {}).setdefault(gidx, []).append(tev)

            def emit_ssq(s):
                tmm = None
                for t in range(NT):
                    tmm = P.op("tensor", lambda e, t_=t: e.matmul(
                        pS[:, t_:t_ + 1], sqb[:, t_ * 128:(t_ + 1) * 128],
                        onescol_bf[:, :], start=True, stop=True),
                        waits=([st_c["sq"], st_c["ssqacc"], tzo] if t == 0 else []),
                        inc=("sT" if t == NT - 1 else None))
                st_c["ssqmm"] = tmm

            for gidx in range(3):
                emit_wload(0, gidx)
                for ch in range(NCH):
                    emit_group(0, gidx, ch)

            for s in range(NS):
                if s + 1 < NS:
                    for gidx in range(3):
                        emit_wload(s + 1, gidx)
                if s >= 1:
                    emit_ssq(s - 1)
                    st_c["ssqacc"] = P.op("vector", lambda e: e.tensor_tensor(
                        ssq[:, :], ssq[:, :], pS[:, 0:32], OP.add),
                        reads=("ssq",), writes=("ssq",),
                        waits=[st_c["ssqmm"], tz3])
                ev_i = evs[s][0]
                ev_f = evs[s][1]
                ev_g = evs[s][2]
                w0 = ev_i + [d_ci]
                P.op("vector", lambda e, s_=s: e.tensor_scalar(
                    C4[:, :], C1[:, 0:T], cicol[:, 4 * s_:4 * s_ + 1], None, OP.mult),
                    reads=("C1", "cicol"), writes=("C4",), waits=w0)
                for k in range(1, 4):
                    P.op("vector", lambda e, s_=s, k_=k: e.scalar_tensor_tensor(
                        C4[:, :], C1[:, k_:k_ + T], cicol[:, 4 * s_ + k_:4 * s_ + k_ + 1],
                        C4[:, :], OP.mult, OP.add),
                        reads=("C1", "C4", "cicol"), writes=("C4",))
                st_c["convi"] = ("sV", P.cnt["sV"])
                w0 = ev_f + [d_cf, st_c["sq"]]
                P.op("vector", lambda e, s_=s: e.tensor_scalar(
                    C5[:, :], C2[:, 0:T], cfcol[:, 4 * s_:4 * s_ + 1], None, OP.mult),
                    reads=("C2", "cfcol"), writes=("C5",), waits=w0)
                for k in range(1, 4):
                    P.op("vector", lambda e, s_=s, k_=k: e.scalar_tensor_tensor(
                        C5[:, :], C2[:, k_:k_ + T], cfcol[:, 4 * s_ + k_:4 * s_ + k_ + 1],
                        C5[:, :], OP.mult, OP.add),
                        reads=("C2", "C5", "cfcol"), writes=("C5",))
                st_c["convf"] = ("sV", P.cnt["sV"])
                t_sg = P.op("scalar", lambda e: e.activation(
                    C3[:, :], C3[:, :], AF.Silu),
                    reads=("C3",), writes=("C3",), waits=ev_g)
                t_silu = P.op("scalar", lambda e: e.activation(
                    C4[:, :], C4[:, :], AF.Silu),
                    reads=("C4",), writes=("C4",), waits=[st_c["convi"]])
                if s + 1 < NS:
                    for ch in range(NCH):
                        emit_group(s + 1, 0, ch)
                t_sig = P.op("scalar", lambda e: e.activation(
                    C5[:, :], C5[:, :], AF.Sigmoid),
                    reads=("C5",), writes=("C5",), waits=[st_c["convf"]])
                if s + 1 < NS:
                    for ch in range(NCH):
                        emit_group(s + 1, 1, ch)
                t_ig = P.op("vector", lambda e: e.scalar_tensor_tensor(
                    C4[:, :], C5[:, :], 1.0, C4[:, :], OP.subtract, OP.mult),
                    reads=("C5", "C4"), writes=("C4",), waits=[t_sig, t_silu])
                st_c["scan"] = P.op("vector", lambda e: e.tensor_tensor_scan(
                    C5[:, :], C5[:, :], C4[:, :], 0.0, OP.mult, OP.subtract),
                    reads=("C5", "C4"), writes=("C5",), waits=[t_ig])
                st_c["sq"] = P.op("scalar", lambda e: e.activation(
                    sqb[:, :], C5[:, :], AF.Square),
                    reads=("C5",), writes=("sqb",),
                    waits=[st_c["scan"], st_c["ssqmm"]])
                sp = s % 2
                st_c["u"] = P.op("vector", lambda e, s_=s, p_=sp: e.scalar_tensor_tensor(
                    ust[p_][:, :], C5[:, :], gnwc[:, s_:s_ + 1], C3[:, :],
                    OP.mult, OP.mult),
                    reads=("C5", "C3", "gnwc"), writes=(f"ust{sp}",),
                    waits=[st_c["scan"], t_sg, d_gn, st_c["udma"][sp]])
                st_c["udma"][sp] = P.dma(
                    "gpsimd", ap3(usp_h, s * 128 * T, [(T, 128), (1, T)]),
                    ust[sp][:, :], "dG0" if sp == 0 else "dG1",
                    waits=[st_c["u"]])
                if s + 1 < NS:
                    for ch in range(NCH):
                        emit_group(s + 1, 2, ch)
            emit_ssq(NS - 1)
            st_c["ssqacc"] = P.op("vector", lambda e: e.tensor_tensor(
                ssq[:, :], ssq[:, :], pS[:, 0:32], OP.add),
                reads=("ssq",), writes=("ssq",), waits=[st_c["ssqmm"], tz3])
            t_rs = P.op("scalar", lambda e: e.activation(
                rms_all[:, :], ssq[:, :], AF.Sqrt, bias=epscol[:, 0:1],
                scale=1.0 / H),
                reads=("ssq", "epscolS"), writes=("rms_all",),
                waits=[st_c["ssqacc"]])
            P.op("vector", lambda e: e.reciprocal(rms_all[:, :], rms_all[:, :]),
                 reads=("rms_all",), writes=("rms_all",), waits=[t_rs])
            if "u" in dbg:
                P.dma("gpsimd", dbg_h["usp"].ap(), usp_h.ap(), "dDBG",
                      waits=[st_c["udma"][0], st_c["udma"][1]])
                P.dma("gpsimd", dbg_h["rms"].ap(), rms_all[:, :], "dDBG")
            P.fence_gpsimd(("dG0", "dG1"))
            P.flush()
        tok_C_st0 = ("dG0", P.cnt["dG0"])
        tok_C_st1 = ("dG1", P.cnt["dG1"])

        if upto == "C":
            return nc

        esD = ExitStack()
        with esD:
            SD = lambda n, sh, dt: esD.enter_context(nc.sbuf_tensor(n, sh, dt))
            A_big = SD("A_bigD", [128, NS * T], BF16)
            qb = [SD(f"qbD{i}", [128, H], BF16) for i in range(2)]
            ptl = [SD(f"ptl{i}", [128, H], F32) for i in range(2)]
            otD = [SD(f"otD{i}", [128, H], F32) for i in range(2)]
            mx8 = [SD(f"mx8{i}", [128, 8], F32) for i in range(2)]
            ngm = [SD(f"ngm{i}", [128, 1], F32) for i in range(2)]
            rsc = [SD(f"rsc{i}", [128, 1], F32) for i in range(2)]
            riv = [SD(f"riv{i}", [128, 1], F32) for i in range(2)]

            tok_addD = [None, None]
            tok_odmaD = [None, None]
            tok_sev = [None, None]
            esC2 = ExitStack()
            with esC2:
                SC2 = lambda n, sh, dt: esC2.enter_context(nc.sbuf_tensor(n, sh, dt))
                wo_bf = SC2("wo_sb", [128, NS * H], BF16)
                ub = [SC2(f"ub{i}", [128, H], BF16) for i in range(2)]
                vbD = SC2("vbD", [128, T], BF16)
                for s in range(NS):
                    P.dma("sync", wo_bf[:, s * H:(s + 1) * H],
                          ap3(wbf_h["wo"], s * 128 * H, [(H, 128), (1, H)]),
                          "dN", waits=list(cast_st["wo"]) if s == 0 else [])
                tok_wo = [("dN", P.cnt["dN"])]
                for s in range(NS):
                    P.dma("sync", A_xn[:, s * T:(s + 1) * T],
                          ap3(qkvT_h[1], s * 128 * T, [(T, 128), (1, T)]),
                          "dK", waits=[tok_B_st0, tok_B_st1])
                tok_KD = ("dK", P.cnt["dK"])
                tok_umm = [None, None]
                tok_oev = [None, None]
                tok_vev = [None, None]
                tok_vtr = None
                vstate = dict(ds=0, kt=0, dv=None)

                def v_step():
                    ds = vstate["ds"]
                    if ds >= NS:
                        return
                    kt0 = vstate["kt"]
                    if kt0 == 0:
                        vstate["dv"] = P.dma(
                            "sync", vbD[:, :],
                            ap3(qkvT_h[2], ds * 128 * T, [(T, 128), (1, T)]),
                            "dNB", waits=[vstate.get("prev_tr"), tok_B_st0, tok_B_st1])
                    g = (kt0 // 8) % 2
                    tk = None
                    for kt in range(kt0, kt0 + 8):
                        tk = P.op("tensor", lambda e, k_=kt, g_=g: e.transpose(
                            pTb[g_][:, (k_ % 8) * 128:(k_ % 8) * 128 + 128],
                            vbD[:, k_ * 128:(k_ + 1) * 128], ident[:, :]),
                            waits=([vstate["dv"], tok_vev[g]] if kt == kt0 else []),
                            inc=("sT" if kt == kt0 + 7 else None))
                    dstv = ap3(A_big, kt0 * H + ds * 128,
                               [(NS * T, 128), (H, 8), (1, 128)])
                    tok_vev[g] = P.op("vector", lambda e, d=dstv, g_=g: e.tensor_copy(
                        d, pTb[g_][:, :].rearrange("p (b c) -> p b c", c=128)),
                        writes=("A_bigD",), waits=[tk])
                    vstate["kt"] = kt0 + 8
                    if vstate["kt"] >= NT:
                        vstate["kt"] = 0
                        vstate["ds"] = ds + 1
                        vstate["prev_tr"] = tk

                for t in range(NT):
                    par = t % 2
                    du = P.dma("sync", ub[par][:, :].rearrange("p (b c) -> p b c", c=128),
                               ap3(usp_h, t * 128, [(T, 128), (128 * T, 8), (1, 128)]),
                               "dU0" if par == 0 else "dU1",
                               waits=[tok_umm[par], tok_C_st0, tok_C_st1])
                    dx = P.dma("sync", ptl[par][:, :],
                               x_h.ap()[t * 128:(t + 1) * 128, :],
                               "dA0" if par == 0 else "dA1", waits=[tok_addD[par]])
                    for half in range(2):
                        mb = fb[half]
                        for s in range(NS):
                            tg = P.op("tensor", lambda e, s_=s, h_=half, b=par, m=mb: e.matmul(
                                m[:, :], ub[b][:, s_ * 128:(s_ + 1) * 128],
                                wo_bf[:, s_ * H + h_ * 512:s_ * H + h_ * 512 + 512],
                                start=(s_ == 0), stop=(s_ == NS - 1)),
                                waits=([du, tok_oev[half]] + tok_wo if s == 0 else []),
                                inc=("sT" if s == NS - 1 else None))
                        tok_oev[half] = P.op("scalar", lambda e, h_=half, b=par, m=mb: e.copy(
                            otD[b][:, h_ * 512:h_ * 512 + 512], m[:, :]),
                            writes=(f"otD{par}",),
                            waits=[tg, tok_odmaD[par]])
                    tok_umm[par] = tg
                    tok_addD[par] = P.op("vector", lambda e, t_=t, b=par: e.scalar_tensor_tensor(
                        otD[b][:, :], otD[b][:, :], rms_all[:, t_:t_ + 1],
                        ptl[b][:, :], OP.mult, OP.add),
                        reads=(f"otD{par}", f"ptl{par}", "rms_all"),
                        writes=(f"otD{par}",),
                        waits=[tok_oev[0], tok_oev[1], dx])
                    tok_odmaD[par] = P.dma(
                        "gpsimd", ap3(out_h, t * 128 * H, [(H, 128), (1, H)]),
                        otD[par][:, :], "dG0" if par == 0 else "dG1",
                        waits=[tok_addD[par]])
                    v_step()
                while vstate["ds"] < NS:
                    v_step()
                tok_sev[0] = tok_oev[0]
                tok_sev[1] = tok_oev[1]
            tok_C2_st0 = ("dG0", P.cnt["dG0"])
            tok_C2_st1 = ("dG1", P.cnt["dG1"])
            S_sb = SD("S_sb", [128, T], F32)
            P_bf = [SD(f"P_bf{i}", [128, T], BF16) for i in range(2)]
            PT_sb = SD("PT_sb", [128, T], BF16)

            tok_smm = [None, None]
            tok_exp_par = [None, None]
            tok_ptr_par = [None, None]
            tok_ptev = [tok_vev[0], tok_vev[1]]
            tok_pv = [None, None]
            tok_pvlast = [None]
            tok_riv = [None, None]
            tok_opv = [None, None]
            tok_chmax_prev = None

            def tail_tr(pv):
                b = pv["par"]
                tk = None
                for kt in range(NT):
                    g = (kt // 8) % 2
                    tk = P.op("tensor", lambda e, k_=kt, g_=g, b_=b: e.transpose(
                        pTb[g_][:, (k_ % 8) * 128:(k_ % 8) * 128 + 128],
                        P_bf[b_][:, k_ * 128:(k_ + 1) * 128], ident[:, :]),
                        waits=([pv["exp"], tok_ptev[g]] if kt % 8 == 0 else []),
                        inc=("sT" if kt % 8 == 7 else None))
                    if kt % 8 == 7:
                        kt0 = kt - 7
                        tok_ptev[g] = P.op("vector", lambda e, k0=kt0, g_=g: e.tensor_copy(
                            PT_sb[:, k0 * 128:(k0 + 8) * 128], pTb[g_][:, :]),
                            writes=("PT_sb",), waits=[tk, tok_pvlast[0]])
                tok_ptr_par[b] = tk

            def tail_pv(pv):
                b = pv["par"]
                tt = pv["t"]
                for half in range(2):
                    mb = fb[2 + half]
                    tg = None
                    for kt in range(NT):
                        tg = P.op("tensor", lambda e, k_=kt, h_=half, m=mb: e.matmul(
                            m[:, :], PT_sb[:, k_ * 128:(k_ + 1) * 128],
                            A_big[:, k_ * H + h_ * 512:k_ * H + h_ * 512 + 512],
                            start=(k_ == 0), stop=(k_ == NT - 1)),
                            waits=([tok_ptev[0], tok_ptev[1], tok_pv[half]]
                                   if kt == 0 else []),
                            inc=("sT" if kt == NT - 1 else None))
                    tok_pvlast[0] = tg
                    tok_pv[half] = P.op("scalar", lambda e, h_=half, b_=b, m=mb: e.activation(
                        otD[b_][:, h_ * 512:h_ * 512 + 512], m[:, :], AF.Copy,
                        scale=riv[b_][:, 0:1]),
                        reads=(f"riv{b}",), writes=(f"otD{b}",),
                        waits=[tg, pv["riv"], tok_odmaD[b]])
                    tok_opv[b] = tok_pv[half]
                tok_addD[b] = P.op("vector", lambda e, b_=b: e.tensor_tensor(
                    otD[b_][:, :], otD[b_][:, :], ptl[b_][:, :], OP.add),
                    reads=(f"otD{b}", f"ptl{b}"), writes=(f"otD{b}",),
                    waits=[tok_pv[0], tok_pv[1], pv["dp"]])
                tok_odmaD[b] = P.dma(
                    "gpsimd", ap3(out_h, tt * 128 * H, [(H, 128), (1, H)]),
                    otD[b][:, :], "dG0" if b == 0 else "dG1",
                    waits=[tok_addD[b]])

            prev = None
            for t in range(NT):
                par = t % 2
                if prev is not None:
                    tail_tr(prev)
                dq = P.dma("sync", qb[par][:, :].rearrange("p (b c) -> p b c", c=128),
                           ap3(qkvT_h[0], t * 128, [(T, 128), (128 * T, 8), (1, 128)]),
                           "dA2" if par == 0 else "dB1",
                           waits=[tok_smm[par], tok_B_st0, tok_B_st1])
                dp = P.dma("sync", ptl[par][:, :],
                           ap3(out_h, t * 128 * H, [(H, 128), (1, H)]),
                           "dP0" if par == 0 else "dP1",
                           waits=[tok_addD[par], tok_C2_st0, tok_C2_st1])
                t_chmax = None
                for kc in range(NCH):
                    sb = fb[kc % 2]
                    tg = None
                    for s in range(NS):
                        tg = P.op("tensor", lambda e, s_=s, k_=kc, b=par, m=sb: e.matmul(
                            m[:, :], qb[b][:, s_ * 128:(s_ + 1) * 128],
                            A_xn[:, s_ * T + k_ * 512:s_ * T + k_ * 512 + 512],
                            start=(s_ == 0), stop=(s_ == NS - 1)),
                            waits=([dq, tok_sev[kc % 2], tok_KD] if s == 0 else []),
                            inc=("sT" if s == NS - 1 else None))
                    tok_smm[par] = tg
                    tok_sev[kc % 2] = P.op("scalar", lambda e, k_=kc, m=sb: e.copy(
                        S_sb[:, k_ * 512:k_ * 512 + 512], m[:, :]),
                        writes=("S_sb",),
                        waits=[tg] + ([tok_chmax_prev] if kc == 0 else []))
                    t_chmax = P.op("vector", lambda e, k_=kc, b=par: e.reduce_max(
                        mx8[b][:, k_:k_ + 1], S_sb[:, k_ * 512:k_ * 512 + 512],
                        axis=AX, negate=True),
                        reads=("S_sb",), writes=(f"mx8{par}",),
                        waits=[tok_sev[kc % 2]])
                tok_chmax_prev = t_chmax
                t_rmx = P.op("vector", lambda e, b=par: e.tensor_reduce(
                    ngm[b][:, 0:1], mx8[b][:, :], AX, OP.min),
                    reads=(f"mx8{par}",), writes=(f"ngm{par}",),
                    waits=[t_chmax, tok_exp_par[par]])
                t_bias = P.op("vector", lambda e, b=par: e.tensor_scalar(
                    ngm[b][:, 0:1], ngm[b][:, 0:1], 1.0 / 32.0, None, OP.mult),
                    reads=(f"ngm{par}",), writes=(f"ngm{par}",), waits=[t_rmx])
                tok_exp = P.op("scalar", lambda e, b=par: e.activation(
                    P_bf[b][:, :], S_sb[:, :], AF.Exp, bias=ngm[b][:, 0:1],
                    scale=1.0 / 32.0, accum_out=rsc[b][:, 0:1]),
                    reads=("S_sb",), writes=(f"P_bf{par}", f"rsc{par}"),
                    waits=[t_bias, tok_ptr_par[par], tok_riv[par]])
                tok_exp_par[par] = tok_exp
                if prev is not None:
                    tail_pv(prev)
                t_riv = P.op("vector", lambda e, b=par: e.reciprocal(
                    riv[b][:, 0:1], rsc[b][:, 0:1]),
                    reads=(f"rsc{par}",), writes=(f"riv{par}",),
                    waits=[tok_exp, tok_opv[par]])
                tok_riv[par] = t_riv
                prev = dict(par=par, t=t, exp=tok_exp, riv=t_riv, dp=dp)
            tail_tr(prev)
            tail_pv(prev)
            P.op("gpsimd", lambda e, v0=P.cnt["dG0"], v1=P.cnt["dG1"]: (
                e.wait_ge(P.sems["dG0"], v0), e.wait_ge(P.sems["dG1"], v1))[0],
                 inc=None)
            P.flush()
    return nc


def prep_inputs(inputs):
    d = inputs
    f32 = np.float32
    x = np.ascontiguousarray(np.asarray(d["L0_attn_x"], f32))
    com = {}
    com["wi"] = np.ascontiguousarray(np.asarray(d["L0_attn_Wi"], f32))
    com["wf"] = np.ascontiguousarray(np.asarray(d["L0_attn_Wf"], f32))
    com["wg"] = np.ascontiguousarray(np.asarray(d["L0_attn_Wg"], f32))
    com["wo"] = np.ascontiguousarray(np.asarray(d["L0_attn_Wo"], f32))
    com["ci"] = np.ascontiguousarray(
        np.asarray(d["L0_attn_conv_i"], f32).reshape(8, 128, 4)
        .transpose(1, 0, 2).reshape(128, 32))
    com["cf"] = np.ascontiguousarray(
        np.asarray(d["L0_attn_conv_f"], f32).reshape(8, 128, 4)
        .transpose(1, 0, 2).reshape(128, 32))
    com["gnw"] = np.ascontiguousarray(
        np.asarray(d["L0_attn_g_norm_weight"], f32).reshape(8, 128).T)
    for f in ("q", "k", "v"):
        com[f + "w1"] = np.ascontiguousarray(
            np.asarray(d[f"L0_ffn_{f}_w1"], f32).transpose(1, 0, 2).reshape(H, H))
        com[f + "b1"] = np.ascontiguousarray(
            np.asarray(d[f"L0_ffn_{f}_b1"], f32).reshape(8, 128).T)
        com[f + "w2"] = np.ascontiguousarray(
            np.asarray(d[f"L0_ffn_{f}_w2"], f32).reshape(H, H))
        b2p = np.zeros((128, H), f32)
        b2p[0:16] = np.asarray(d[f"L0_ffn_{f}_b2"], f32)
        com[f + "b2"] = b2p
        nwp = np.zeros((H, 16), f32)
        nwp[:, 0:15] = np.asarray(d[f"L0_ffn_{f}_node_w"], f32).T
        com[f + "nwT"] = nwp
        nbp = np.zeros((1, 16), f32)
        nbp[0, 0:15] = np.asarray(d[f"L0_ffn_{f}_node_b"], f32)
        com[f + "nb"] = nbp
    com["ident"] = np.eye(128, dtype=BF)
    ep = np.zeros((128, H), dtype=BF)
    for p in range(8):
        ep[2 * p, p * 128:p * 128 + 64] = 1
        ep[2 * p + 1, p * 128 + 64:p * 128 + 128] = 1
    com["epairs"] = ep
    com["iota16"] = np.tile(np.arange(16, dtype=f32), (128, 1))
    com["onesrow"] = np.ones((1, 128), dtype=BF)
    maps = []
    for c in range(NCORE):
        m = dict(com)
        m["x"] = np.ascontiguousarray(x[c])
        maps.append(m)
    return maps


def kernel(**inputs):
    maps = prep_inputs(inputs)
    nc = build_program()
    res = run_bass_kernel_spmd(nc, maps, core_ids=list(range(NCORE)))
    return np.stack([r["out"] for r in res.results], axis=0)
